# version 15
# baseline (speedup 1.0000x reference)
import math
import numpy as np
import ml_dtypes
from contextlib import ExitStack

import concourse.bass as bass
import concourse.mybir as mybir

F32 = mybir.dt.float32
F32R = mybir.dt.float32r
BF16 = mybir.dt.bfloat16
OP = mybir.AluOpType
AX = mybir.AxisListType.X
ACT = mybir.ActivationFunctionType

D = 2048
NH = 16
HD = 128
CH = 128
NDC = D // 128
HID = 5461
HIDP = 5504
NHC = HIDP // 128
HTAIL = HID - 42 * 128


def _build_slopes(n=NH):
    def pow2(n):
        start = 2.0 ** (-(2.0 ** (-(math.log2(n) - 3))))
        return [start * start ** i for i in range(n)]
    assert math.log2(n).is_integer()
    return np.asarray(pow2(n), dtype=np.float32)


def _decay_constants():
    slopes = _build_slopes()
    r32 = np.exp(-slopes.astype(np.float64)).astype(np.float32)
    r = r32.astype(np.float64)
    i = np.arange(CH, dtype=np.float64)
    q_decay = (r[:, None] ** (i[None, :] + 1.0)).astype(np.float32)
    k_decay = (r[:, None] ** (CH - 1.0 - i[None, :])).astype(np.float32)
    diff = i[:, None] - i[None, :]
    mask = diff >= 0
    dmask = np.where(mask[None, :, :],
                     r[:, None, None] ** np.where(mask, diff, 0.0)[None],
                     0.0).astype(np.float32)
    r_C = (r ** float(CH)).astype(np.float32)
    return dict(r=r32, q_decay=q_decay, k_decay=k_decay, dmask=dmask, r_C=r_C)


def _rotary_tables(t0, T):
    freq = ((1.0 / 1024.0) ** np.linspace(0.0, 1.0, HD // 4)).astype(np.float32)
    freq = np.concatenate([freq, np.zeros(HD // 4, dtype=np.float32)])
    t = np.arange(t0, t0 + T, dtype=np.float32)
    theta = (t[:, None] * freq[None, :]).astype(np.float32)
    cos = np.cos(theta.astype(np.float64)).astype(np.float32)
    sin = np.sin(theta.astype(np.float64)).astype(np.float32)
    return cos, sin

ENGS = ("sync", "gpsimd", "tensor", "vector", "scalar")


def _run_op(e, op):
    k = op[0]
    if k == "mm":
        return e.matmul(op[1], op[2], op[3], start=op[4], stop=op[5])
    if k == "tr":
        return e.transpose(op[1], op[2], op[3])
    if k == "copy":
        return e.tensor_copy(op[1], op[2])
    if k == "scopy":
        return e.copy(op[1], op[2])
    if k == "tt":
        return e.tensor_tensor(op[1], op[2], op[3], op[4])
    if k == "ts":
        return e.tensor_scalar(op[1], op[2], op[3], op[4], op[5], op[6])
    if k == "tss":
        return e.tensor_single_scalar(op[1], op[2], op[3], op[4])
    if k == "stt":
        return e.scalar_tensor_tensor(op[1], op[2], op[3], op[4], op[5], op[6])
    if k == "red":
        return e.tensor_reduce(op[1], op[2], AX, op[3], apply_absolute_value=op[4])
    if k == "act":
        return e.activation(op[1], op[2], op[3], bias=op[4], scale=op[5])
    if k == "recip":
        return e.reciprocal(op[1], op[2])
    if k == "memset":
        return e.memset(op[1], op[2])
    if k == "dma":
        return e.dma_start(out=op[1], in_=op[2])
    if k == "cc":
        return e.collective_compute("AllGather", OP.bypass,
                                    replica_groups=op[1], ins=[op[2]], outs=[op[3]])
    if k == "drain":
        return e.drain()
    if k == "nop":
        return e.nop()
    raise ValueError(k)


class Plan:
    def __init__(self, nc):
        self.nc = nc
        self.ops = []
        self.ctr = {}
        self.bank_w = {}
        self.bank_r = {}
        self.pending = {}

    def cur(self, s):
        return self.ctr.get(s.name, 0)

    def emit(self, eng, op, waits=(), incs=()):
        waits = [w for w in waits if w is not None]
        if eng in self.pending and op[0] != "nop":
            waits = self.pending.pop(eng) + waits
        incs = [i for i in incs if i is not None]
        for s, a in incs:
            self.ctr[s.name] = self.ctr.get(s.name, 0) + a
        self.ops.append((eng, op, waits, incs))

    def defer(self, eng, waits):
        self.pending.setdefault(eng, []).extend(
            [w for w in waits if w is not None])

    def flush(self, tag=None):
        ops, self.ops = self.ops, []
        if not ops:
            return
        import contextlib
        i_lo = self.nc.n_instructions()
        scope = self.nc.named_scope(tag) if tag else contextlib.nullcontext()
        with scope, self.nc.Block() as blk:
            for eng in ENGS:
                mine = [o for o in ops if o[0] == eng]
                if not mine:
                    continue

                def mk(mine=mine):
                    def section(e):
                        for _, op, waits, incs in mine:
                            for s, v in waits:
                                e.wait_ge(s, v)
                            inst = _run_op(e, op)
                            for s, a in incs:
                                inst.then_inc(s, a)
                    return section
                getattr(blk, eng)(mk())
        if not hasattr(self.nc, "_phase_ranges"):
            self.nc._phase_ranges = []
        self.nc._phase_ranges.append((tag, i_lo, self.nc.n_instructions()))

    def wbank(self, key):
        return self.bank_r.get(key)

    def set_w(self, key, sv):
        self.bank_w[key] = sv

    def rbank(self, key):
        return self.bank_w.get(key)

    def set_r(self, key, sv):
        self.bank_r[key] = sv


def build(NCH=8, NCORES=8, TQ=256, debug_outputs=()):
    TPC = NCH * CH
    NTQ = TPC // TQ
    SL = min(512, TPC)
    NSL = TPC // SL
    NTT = TQ // 128

    cdc = _decay_constants()
    r_C = [float(x) for x in cdc["r_C"]]

    nc = bass.Bass("TRN2", target_bir_lowering=False)

    ap = {}

    def ein(name, shape, dt=F32):
        ap[name] = nc.dram_tensor(name, shape, dt, kind="ExternalInput").ap()

    ein("x", [TPC, D])
    ein("wq", [D, D]); ein("wk", [D, D]); ein("wv", [D, D]); ein("wo", [D, D])
    ein("rmsw_row", [1, D])
    S1 = -(-NHC // NCORES)
    S2 = -(-NDC // NCORES)
    TN = 2 * S1 * NDC + S2 * NHC
    ein("w1s", [S1 * 128, D]); ein("w3s", [S1 * 128, D])
    ein("w2s", [S2 * 128, HID])
    ein("identf", [128, 128])
    ein("identb", [128, 128], BF16)
    ein("swap64", [128, 128])
    ein("rotA", [128, TPC])
    ein("rotB", [128, TPC])
    ein("dmaskT", [128, NH * CH])
    ein("qdec", [128, NH * CH])
    ein("kdec", [128, NH * CH])
    ein("coef", [128, NCORES * NH])
    ein("onescol", [128, 1])
    ein("onesrow", [1, 128])

    out_t = nc.dram_tensor("out", [TPC, D], F32, kind="ExternalOutput").ap()
    dbg = {}
    for nm in debug_outputs:
        shp = {"q": [NCH, 128, D], "k": [NCH, 128, D], "v": [NCH, 128, D],
               "qrot": [NCH, 128, D], "krot": [NCH, 128, D],
               "attn": [NCH, 128, D], "attn_n": [NCH, 128, D],
               "y1": [NDC, 128, TPC], "x2q": [NTQ, NDC, 128, TQ],
               "gq": [NTQ, NHC, 128, TQ], "sall": [NCORES * 128, NH * HD],
               "kvin": [128, NH * HD], "xfm": [NDC, 128, TPC],
               "svals": [1, 8], "t1q": [NHC, NDC, 128, 128],
               "ln1": [1, 2 * 1024]}[nm]
        dbg[nm] = nc.dram_tensor("dbg_" + nm, shp, F32, kind="ExternalOutput").ap()

    x_fm_d = nc.dram_tensor("x_fm_d", [NDC, 128, TPC], F32, kind="Internal").ap()
    q_d = nc.dram_tensor("q_d", [NCH, 128, D], F32, kind="Internal").ap()
    k_d = nc.dram_tensor("k_d", [NCH, 128, D], F32, kind="Internal").ap()
    v_d = nc.dram_tensor("v_d", [NCH, 128, D], F32, kind="Internal").ap()
    qt_d = nc.dram_tensor("qt_d", [NCH, 128, D], F32, kind="Internal").ap()
    afm_d = nc.dram_tensor("afm_d", [NDC, 128, TPC], F32, kind="Internal").ap()
    y1_d = nc.dram_tensor("y1_d", [NDC, 128, TPC], F32, kind="Internal").ap()
    AS = "Shared" if NCORES > 4 else "Local"
    ln1_d = nc.dram_tensor("ln1_d", [1, 2 * 1024], F32, kind="Internal").ap()
    cc_in = nc.dram_tensor("cc_in", [128, NH * HD], F32, kind="Internal").ap()
    cc_out = nc.dram_tensor("cc_out", [NCORES * 128, NH * HD], F32, kind="Internal", addr_space=AS).ap()
    cc2_in = nc.dram_tensor("cc2_in", [1, 4], F32, kind="Internal").ap()
    cc2_out = nc.dram_tensor("cc2_out", [NCORES, 4], F32, kind="Internal", addr_space=AS).ap()
    cct_in = nc.dram_tensor("cct_in", [TN, 128, 128], BF16, kind="Internal").ap()
    cct_out = nc.dram_tensor("cct_out", [NCORES, TN, 128, 128], BF16,
                             kind="Internal", addr_space=AS).ap()

    with ExitStack() as ctx:
        def sbuf(name, shape, dt=F32):
            return ctx.enter_context(nc.sbuf_tensor(name, shape, dt))

        def sem(name):
            return ctx.enter_context(nc.semaphore(name))

        c_identf = sbuf("c_identf", [128, 128])
        c_identb = sbuf("c_identb", [128, 128], BF16)
        c_swap = sbuf("c_swap", [128, 128])
        c_onesc = sbuf("c_onesc", [128, 1])
        c_onesr = sbuf("c_onesr", [1, 128])
        stage = sbuf("stage", [128, D])
        colbuf = sbuf("colbuf", [128, 16])
        svals = sbuf("svals", [1, 16])
        scol = sbuf("scol", [128, 12])
        bca = sbuf("bca", [128, TQ])
        bcb = sbuf("bcb", [128, TQ])
        gtmp = sbuf("gtmp", [128, TQ])
        gtmp2 = sbuf("gtmp2", [128, TQ])

        pbs = [ctx.enter_context(nc.psum_tensor(f"pb{i}", [128, 512], F32))
               for i in range(8)]

        s_cin = sem("s_cin")
        s_w = sem("s_w")
        s_w2 = sem("s_w2")
        s_w3 = sem("s_w3")
        s_w4 = sem("s_w4")
        s_w5 = sem("s_w5")
        s_w6 = sem("s_w6")
        s_ld4 = sem("s_ld4")
        s_ld1 = sem("s_ld1")
        s_ld2 = sem("s_ld2")
        s_ld3 = sem("s_ld3")
        s_ldA = sem("s_ldA")
        s_spA = sem("s_spA")
        s_spA2 = sem("s_spA2")
        s_spB = sem("s_spB")
        s_spC = sem("s_spC")
        s_spD = sem("s_spD")
        s_spE = sem("s_spE")
        s_spF = sem("s_spF")
        s_dbg = sem("s_dbg")
        s_outd = sem("s_outd")
        s_cast = sem("s_cast")
        s_cast2 = sem("s_cast2")
        s_mm = sem("s_mm")
        s_ve = sem("s_ve")
        s_sc = sem("s_sc")
        s_gp = sem("s_gp")
        s_cc = sem("s_cc")

        pl = Plan(nc)

        def MM(bank_ap, key, lhsT, rhs, start, stop, waits=(), mark=False):
            w = list(waits)
            if start:
                w.append(pl.wbank(key))
            incs = [(s_mm, 1)] if (stop or mark) else []
            pl.emit("tensor", ("mm", bank_ap, lhsT, rhs, start, stop), w, incs)
            if stop:
                pl.set_w(key, (s_mm, pl.cur(s_mm)))

        def TR(bank_ap, key, in_ap, ident, waits=()):
            w = list(waits)
            w.append(pl.wbank(key))
            pl.emit("tensor", ("tr", bank_ap, in_ap, ident), w, [(s_mm, 1)])
            pl.set_w(key, (s_mm, pl.cur(s_mm)))

        def EVAC(eng, op, key, waits=()):
            w = list(waits)
            w.append(pl.rbank(key))
            isem = s_ve if eng == "vector" else s_sc
            pl.emit(eng, op, w, [(isem, 1)])
            pl.set_r(key, (isem, pl.cur(isem)))
            return (isem, pl.cur(isem))

        def V(op, waits=(), inc=True):
            pl.emit("vector", op, waits, [(s_ve, 1)] if inc else [])
            return (s_ve, pl.cur(s_ve))

        def VD():
            pl.emit("vector", ("drain",), [], [])

        def SC(op, waits=(), inc=True):
            pl.emit("scalar", op, waits, [(s_sc, 1)] if inc else [])
            return (s_sc, pl.cur(s_sc))

        def SCD():
            pl.emit("scalar", ("drain",), [], [])

        def G(op, waits=(), inc=True):
            pl.emit("gpsimd", op, waits, [(s_gp, 1)] if inc else [])
            return (s_gp, pl.cur(s_gp))

        def GPD():
            pl.emit("gpsimd", ("drain",), [], [])

        def GD(out_ap, in_ap, semx, waits=()):
            pl.emit("gpsimd", ("dma", out_ap, in_ap), waits, [(semx, 16)])
            return (semx, pl.cur(semx))

        def GC(out_ap, in_ap, waits=()):
            pl.emit("gpsimd", ("dma", out_ap, in_ap), waits, [(s_cast, 16)])
            return (s_cast, pl.cur(s_cast))

        def SY(out_ap, in_ap, waits=(), semx=None):
            semx = semx or s_cin
            pl.emit("sync", ("dma", out_ap, in_ap), waits, [(semx, 16)])
            return (semx, pl.cur(semx))

        def SCY(out_ap, in_ap, waits=(), semx=None):
            semx = semx or s_w5
            pl.emit("scalar", ("dma", out_ap, in_ap), waits, [(semx, 16)])
            return (semx, pl.cur(semx))

        def FENCE(*waits):
            pl.emit("vector", ("nop",), list(waits), [])

        def FENCE_ALL():
            allsems = [s_cin, s_w, s_w2, s_w3, s_w4, s_w5, s_w6, s_ld4, s_ld1, s_ld2, s_ld3, s_ldA, s_spA,
                       s_spA2, s_spB, s_spC, s_spD, s_spE, s_spF, s_dbg, s_outd,
                       s_cast, s_cast2]
            pl.emit("vector", ("nop",),
                    [(x, pl.cur(x)) for x in allsems if pl.cur(x) > 0], [])
            pl.flush("B0")

        def bc_heads(const_ap, reps):
            return const_ap.unsqueeze(1).broadcast_to([128, reps, 128])

        def bcast_row(row_ap, width, bank_key, dst_ap, waits=()):
            done = None
            for off in range(0, width, 512):
                wd = min(512, width - off)
                MM(pbs[bank_key][:, 0:wd], bank_key, c_onesr[:],
                   row_ap[:, off:off + wd], True, True, waits=waits if off == 0 else ())
                done = EVAC("vector",
                            ("copy", dst_ap[:, off:off + wd], pbs[bank_key][:, 0:wd]),
                            bank_key)
            return done

        for nm, dst in [("identf", c_identf), ("identb", c_identb),
                        ("swap64", c_swap), ("onescol", c_onesc),
                        ("onesrow", c_onesr)]:
            SY(dst[:], ap[nm])
        FENCE((s_cin, pl.cur(s_cin)))
        pl.flush("B0")

        if True:

            b8l_ctx = ExitStack()
            wn8 = [b8l_ctx.enter_context(nc.sbuf_tensor(f"wn8{i}", [128, D], F32))
                   for i in range(2)]
            wq8 = [b8l_ctx.enter_context(nc.sbuf_tensor(f"wq8{i}", [128, D], BF16))
                   for i in range(2)]
            wst8 = [b8l_ctx.enter_context(nc.sbuf_tensor(f"wst8{i}", [128, D], BF16))
                    for i in range(2)]
            ws8 = b8l_ctx.enter_context(nc.sbuf_tensor("ws8", [128, D], F32))
            svp_sb = b8l_ctx.enter_context(nc.sbuf_tensor("svp_sb", [128, 4], F32))
            xhl_ctx = ExitStack()
            xhi2 = xhl_ctx.enter_context(nc.sbuf_tensor("xhi2", [128, NDC * TPC], BF16))
            xlo2 = xhl_ctx.enter_context(nc.sbuf_tensor("xlo2", [128, NDC * TPC], BF16))
            xfm_ctx = ExitStack()
            xfm = xfm_ctx.enter_context(nc.sbuf_tensor("xfm", [128, NDC * TPC], F32))
            stage2 = xfm_ctx.enter_context(nc.sbuf_tensor("xstg2", [128, D], F32))
            if True:
                def xfm_t(dc):
                    return xfm[:][:, dc * TPC:(dc + 1) * TPC]

                for tb in range(NCH):
                    lx = SY(stage2[:], ap["x"][tb * 128:(tb + 1) * 128, :],
                            semx=s_w)
                    for dc in range(NDC):
                        key = (tb * NDC + dc) % 4
                        TR(pbs[key][:, 0:128], key,
                           stage2[:][:, dc * 128:(dc + 1) * 128],
                           c_identf[:], waits=[lx] if dc == 0 else ())
                        EVAC("vector",
                             ("copy", xfm_t(dc)[:, tb * 128:(tb + 1) * 128],
                              pbs[key][:, 0:128]), key)
                    pl.defer("sync", [(s_mm, pl.cur(s_mm))])
                xfmv = (s_ve, pl.cur(s_ve))
                for dc in range(NDC):
                    GD(x_fm_d[dc], xfm_t(dc), s_spA, waits=[xfmv] if dc == 0 else ())
                if "xfm" in dbg:
                    for dc in range(NDC):
                        GD(dbg["xfm"][dc], xfm_t(dc), s_dbg)
                pl.flush("B1")

                for dc in range(NDC):
                    V(("copy", xhi2[:][:, dc * TPC:(dc + 1) * TPC], xfm_t(dc)),
                      waits=[xfmv] if dc == 0 else ())
                    VD()
                    V(("tt", xlo2[:][:, dc * TPC:(dc + 1) * TPC], xfm_t(dc),
                       xhi2[:][:, dc * TPC:(dc + 1) * TPC], OP.subtract))
                    VD()
                xsplv = (s_ve, pl.cur(s_ve))
                pl.flush("B1s")
            FENCE_ALL()
            xfm_ctx.close()

            mats = [("w1s", HID, D, 0), ("w3s", HID, D, S1 * NDC),
                    ("w2s", D, HID, 2 * S1 * NDC)]
            b8_steps = []
            b8_when = []
            b8_state = dict(red={0: None, 1: None},
                            q={0: None, 1: None}, tr={0: None, 1: None},
                            st={0: None, 1: None}, scolv=None, pc=0)

            def _mk_passA(mi, wt, sl_, c0, c1):
                def step():
                    st_ = b8_state
                    par = st_["pc"] % 2
                    st_["pc"] += 1
                    lw = SCY(wn8[par][:, 0:c1 - c0],
                             ap[wt][sl_ * 128:(sl_ + 1) * 128, c0:c1],
                             semx=s_w5 if par == 0 else s_w6,
                             waits=[st_["red"][par]])
                    V(("red", colbuf[:][:, 0:1], wn8[par][:][:, 0:c1 - c0],
                       OP.add, True), waits=[lw])
                    VD()
                    V(("tt", colbuf[:][:, 4 + mi:5 + mi],
                       colbuf[:][:, 4 + mi:5 + mi], colbuf[:][:, 0:1], OP.add))
                    VD()
                    st_["red"][par] = (s_ve, pl.cur(s_ve))
                return step

            def _mk_partial_cc():
                def step():
                    for mi, (wt, nrow, ncol, _) in enumerate(mats):
                        MM(pbs[7][0:1, 0:1], 7, c_onesc[:],
                           colbuf[:][:, 4 + mi:5 + mi], True, True,
                           waits=[(s_ve, pl.cur(s_ve))] if mi == 0 else ())
                        EVAC("vector", ("copy", svals[:][:, mi:mi + 1],
                                        pbs[7][0:1, 0:1]), 7)
                        VD()
                    gdp = GD(cc2_in, svals[:][0:1, 0:4], s_spB,
                             waits=[(s_ve, pl.cur(s_ve))])
                    pl.emit("gpsimd",
                            ("cc", [list(range(NCORES))], cc2_in, cc2_out),
                            [gdp], [(s_cc, 1)])
                return step

            def _mk_scalc_final():
                def step():
                    pl.emit("scalar", ("dma", svp_sb[0:NCORES, 0:4], cc2_out),
                            [(s_cc, pl.cur(s_cc))], [(s_ld4, 16)])
                    MM(pbs[7][0:1, 0:4], 7, c_onesc[0:NCORES, 0:1],
                       svp_sb[0:NCORES, 0:4], True, True,
                       waits=[(s_ld4, pl.cur(s_ld4))])
                    EVAC("vector", ("copy", svals[:][:, 0:4],
                                    pbs[7][0:1, 0:4]), 7)
                    VD()
                    for mi, (wt, nrow, ncol, _) in enumerate(mats):
                        V(("ts", svals[:][:, 4 + mi:5 + mi],
                           svals[:][:, mi:mi + 1],
                           1.0 / (nrow * ncol), None, OP.mult, OP.bypass))
                        VD()
                        V(("ts", svals[:][:, 8 + mi:9 + mi],
                           svals[:][:, 4 + mi:5 + mi],
                           0.5, None, OP.mult, OP.bypass))
                        VD()
                    for mi in range(3):
                        bcast_row(svals[:][:, 4 + mi:5 + mi], 1, 7,
                                  scol[:][:, mi:mi + 1],
                                  waits=[(s_ve, pl.cur(s_ve))] if mi == 0
                                  else ())
                        bcast_row(svals[:][:, 8 + mi:9 + mi], 1, 7,
                                  scol[:][:, 3 + mi:4 + mi])
                        V(("ts", scol[:][:, 6 + mi:7 + mi],
                           scol[:][:, 3 + mi:4 + mi],
                           -1.0, None, OP.mult, OP.bypass),
                          waits=[(s_ve, pl.cur(s_ve))])
                        VD()
                    b8_state["scolv"] = (s_ve, pl.cur(s_ve))
                return step

            def _mk_passB(mi, wt, sl_, c0, c1, t0, base):
                def step():
                    st_ = b8_state
                    par = st_["pc"] % 2
                    st_["pc"] += 1
                    ntp = (c1 - c0 + 127) // 128
                    ncp = ntp * 128
                    prev = st_["q"][par] or st_["red"][par]
                    mzv = None
                    if c1 - c0 < ncp:
                        mzv = V(("memset", wn8[par][:, c1 - c0:ncp], 0.0),
                                waits=[prev])
                        VD()
                    lw = SCY(wn8[par][:, 0:c1 - c0],
                             ap[wt][sl_ * 128:(sl_ + 1) * 128, c0:c1],
                             semx=s_w5 if par == 0 else s_w6,
                             waits=[mzv if mzv else prev])
                    V(("tss", ws8[:][:, 0:ncp], wn8[par][:][:, 0:ncp],
                       scol[:][:, 6 + mi:7 + mi], OP.is_lt),
                      waits=[lw, st_["tr"][par], st_["scolv"]])
                    VD()
                    qv = V(("stt", wq8[par][:][:, 0:ncp], wn8[par][:][:, 0:ncp],
                            scol[:][:, 3 + mi:4 + mi], ws8[:][:, 0:ncp],
                            OP.is_gt, OP.subtract))
                    st_["q"][par] = qv
                    for ti in range(ntp):
                        bk = 6 + ti % 2
                        TR(pbs[bk][:, 0:512].bitcast(BF16)[:, 0:128], bk,
                           wq8[par][:][:, ti * 128:(ti + 1) * 128], c_identb[:],
                           waits=[w for w in [qv if ti == 0 else None,
                                              st_["st"][par] if ti == 0 else None]
                                  if w])
                        eng = "scalar" if ti % 2 == 0 else "vector"
                        EVAC(eng, (("scopy" if ti % 2 == 0 else "copy"),
                                   wst8[par][:][:, ti * 128:(ti + 1) * 128],
                                   pbs[bk][:, 0:512].bitcast(BF16)[:, 0:128]), bk)
                    st_["tr"][par] = (s_mm, pl.cur(s_mm))
                    stv = [(s_sc, pl.cur(s_sc)), (s_ve, pl.cur(s_ve))]
                    gv = GD(cct_in[base + t0:base + t0 + ntp]
                            .rearrange("a p t -> p a t"),
                            wst8[par][:][:, 0:ncp]
                            .rearrange("p (a t) -> p a t", a=ntp),
                            s_spD if par == 0 else s_spE, waits=stv)
                    st_["st"][par] = gv
                return step

            V(("memset", svals[:], 0.0))
            for mi in range(3):
                V(("memset", colbuf[:][:, 4 + mi:5 + mi], 0.0))
            pl.flush("B8z")
            stepsA = []
            for mi, (wt, nrow, ncol, _) in enumerate(mats):
                nsl = S1 if mi < 2 else S2
                for sl_ in range(nsl):
                    for c0 in range(0, ncol, D):
                        stepsA.append(_mk_passA(mi, wt, sl_, c0,
                                                min(c0 + D, ncol)))
            stepsA.append(_mk_partial_cc())
            stepsB = []
            for mi, (wt, nrow, ncol, base) in enumerate(mats):
                nsl = S1 if mi < 2 else S2
                npc = NHC if mi == 2 else NDC
                for sl_ in range(nsl):
                    for pidx, c0 in enumerate(range(0, ncol, D)):
                        stepsB.append(_mk_passB(mi, wt, sl_, c0,
                                                min(c0 + D, ncol),
                                                pidx * (D // 128),
                                                base + sl_ * npc))
            for i in range(len(stepsA)):
                b8_steps.append(stepsA[i])
                b8_when.append(i // (-(-len(stepsA) // 4)))
            b8_steps.append(_mk_scalc_final())
            b8_when.append(14)
            for i in range(len(stepsB)):
                b8_steps.append(stepsB[i])
                b8_when.append(15 + i // (-(-len(stepsB) // 8)))
            b8_i = [0]

            def b8_pump(g8):
                while (b8_i[0] < len(b8_steps)
                       and (g8 is None or b8_when[b8_i[0]] <= g8)):
                    b8_steps[b8_i[0]]()
                    b8_i[0] += 1

            w8_ctx = ExitStack()
            w8f = [w8_ctx.enter_context(nc.sbuf_tensor(f"w8f{i}", [128, NDC * 256], F32))
                   for i in range(2)]
            w8h = [w8_ctx.enter_context(nc.sbuf_tensor(f"w8h{i}", [128, NDC * 256], BF16))
                   for i in range(2)]
            w8l = [w8_ctx.enter_context(nc.sbuf_tensor(f"w8l{i}", [128, NDC * 256], BF16))
                   for i in range(2)]
            stg2b = w8_ctx.enter_context(nc.sbuf_tensor("stg2b", [128, D], F32))
            if True:
                spills = {0: None, 1: None}
                jobs = []
                for wt, dst_d in [("wq", q_d), ("wk", k_d), ("wv", v_d)]:
                    for e8 in range(8):
                        jobs.append((wt, dst_d, e8))
                NJ = len(jobs)
                lwh = {}
                spldone = {}
                mmcons = {}

                def b2_load(j):
                    wt, _, e8 = jobs[j]
                    par = j % 2
                    lwh[j] = SY(w8f[par][:].rearrange("p (c e) -> p c e", c=NDC),
                                ap[wt][:, e8 * 256:(e8 + 1) * 256]
                                .rearrange("(c p) e -> p c e", p=128),
                                semx=s_w if par == 0 else s_w2,
                                waits=[spldone.get(j - 2)])

                def b2_split(j):
                    par = j % 2
                    V(("copy", w8h[par][:], w8f[par][:]),
                      waits=[lwh[j], mmcons.get(j - 2)])
                    VD()
                    V(("tt", w8l[par][:], w8f[par][:], w8h[par][:],
                       OP.subtract))
                    VD()
                    spldone[j] = (s_ve, pl.cur(s_ve))

                b2_load(0)
                b2_load(1)
                b2_split(0)
                for j in range(NJ):
                    wt, dst_d, e8 = jobs[j]
                    par = j % 2
                    if j + 2 < NJ:
                        b2_load(j + 2)
                    if j + 1 < NJ:
                        b2_split(j + 1)
                    for ebl2 in range(2):
                        eb = e8 * 2 + ebl2
                        ebpar = eb % 2
                        stg = stage if ebpar == 0 else stg2b
                        for sl in range(NSL):
                            bk = ebpar * 2 + sl
                            for dc in range(NDC):
                                wtile_h = w8h[par][:][:, dc * 256 + ebl2 * 128:
                                                      dc * 256 + (ebl2 + 1) * 128]
                                wtile_l = w8l[par][:][:, dc * 256 + ebl2 * 128:
                                                      dc * 256 + (ebl2 + 1) * 128]
                                xh = xhi2[:][:, dc * TPC + sl * SL:
                                             dc * TPC + (sl + 1) * SL]
                                xl = xlo2[:][:, dc * TPC + sl * SL:
                                             dc * TPC + (sl + 1) * SL]
                                MM(pbs[bk][:, 0:SL], bk, wtile_h, xh,
                                   dc == 0, False,
                                   waits=[spldone[j], xsplv] if dc == 0 else ())
                                MM(pbs[bk][:, 0:SL], bk, wtile_h, xl,
                                   False, False)
                                MM(pbs[bk][:, 0:SL], bk, wtile_l, xh,
                                   False, dc == NDC - 1)
                            EVAC("vector",
                                 ("copy", stg[:][:, sl * SL:(sl + 1) * SL],
                                  pbs[bk][:, 0:SL]), bk,
                                 waits=[spills[ebpar]]
                                 if (sl == 0 and spills[ebpar]) else ())
                        ev = (s_ve, pl.cur(s_ve))
                        gv = GD(dst_d[:, :, eb * 128:(eb + 1) * 128]
                                .rearrange("c p t -> p c t"),
                                stg[:][:, 0:TPC]
                                .rearrange("p (c t) -> p c t", c=NCH),
                                s_spA if ebpar == 0 else s_spA2,
                                waits=[ev])
                        spills[ebpar] = gv
                    mmcons[j] = (s_mm, pl.cur(s_mm))
                    b8_pump(j)
                    if j % 4 == 3:
                        pl.flush(f"B2p{j // 4}")
                if any(n in dbg for n in ("q", "k", "v")):
                    for nm, src in [("q", q_d), ("k", k_d), ("v", v_d)]:
                        if nm in dbg:
                            GD(dbg[nm].rearrange("a b c -> (a b) c"),
                               src.rearrange("a b c -> (a b) c"), s_dbg)
                    pl.flush("B2d")
            FENCE_ALL()
            w8_ctx.close()
            while b8_i[0] < len(b8_steps):
                nxt = b8_i[0] + 6
                while b8_i[0] < min(nxt, len(b8_steps)):
                    b8_steps[b8_i[0]]()
                    b8_i[0] += 1
                pl.flush("B8r")
            FENCE_ALL()
            xhl_ctx.close()
            b8l_ctx.close()

            attn_ctx = ExitStack()
            c_dmaskT = attn_ctx.enter_context(nc.sbuf_tensor("c_dmaskT", [128, NH * CH], F32))
            c_qdec = attn_ctx.enter_context(nc.sbuf_tensor("c_qdec", [128, NH * CH], F32))
            c_kdec = attn_ctx.enter_context(nc.sbuf_tensor("c_kdec", [128, NH * CH], F32))
            c_rotA = attn_ctx.enter_context(nc.sbuf_tensor("c_rotA", [128, TPC], F32))
            c_rotB = attn_ctx.enter_context(nc.sbuf_tensor("c_rotB", [128, TPC], F32))
            c_coef = attn_ctx.enter_context(nc.sbuf_tensor("c_coef", [128, NCORES * NH], F32))
            c_rmsw = attn_ctx.enter_context(nc.sbuf_tensor("c_rmsw", [1, D], F32))
            pkv = attn_ctx.enter_context(nc.sbuf_tensor("pkv", [128, NH * HD], F32))
            for nm, dst in [("dmaskT", c_dmaskT), ("qdec", c_qdec),
                            ("kdec", c_kdec), ("rotA", c_rotA), ("rotB", c_rotB),
                            ("coef", c_coef), ("rmsw_row", c_rmsw)]:
                SY(dst[:], ap[nm])
            cin0 = (s_cin, pl.cur(s_cin))
            FENCE(cin0)
            pl.flush("B2")
            atm_ctx = ExitStack()
            atm = atm_ctx.enter_context(nc.sbuf_tensor("atm", [128, NCH * D], F32))
            if True:
                def atm_t(c):
                    return atm[:][:, c * D:(c + 1) * D]

                chunk_ctx = ExitStack()
                qch = chunk_ctx.enter_context(nc.sbuf_tensor("qch", [128, D], F32))
                kch = chunk_ctx.enter_context(nc.sbuf_tensor("kch", [128, D], F32))
                vch = chunk_ctx.enter_context(nc.sbuf_tensor("vch", [128, D], F32))
                qtch = chunk_ctx.enter_context(nc.sbuf_tensor("qtch", [128, D], F32))
                ktch = chunk_ctx.enter_context(nc.sbuf_tensor("ktch", [128, D], F32))
                kttm = chunk_ctx.enter_context(nc.sbuf_tensor("kttm", [128, D], F32))
                vtm = chunk_ctx.enter_context(nc.sbuf_tensor("vtm", [128, D], F32))
                stage2 = chunk_ctx.enter_context(nc.sbuf_tensor("rstg2", [128, D], F32))
                qch2 = chunk_ctx.enter_context(nc.sbuf_tensor("qch2", [128, D], F32))
                kch2 = chunk_ctx.enter_context(nc.sbuf_tensor("kch2", [128, D], F32))
                if True:
                    FENCE((s_spA, pl.cur(s_spA)), (s_spA2, pl.cur(s_spA2)))
                    V(("memset", pkv[:], 0.0))
                    pzero = (s_ve, pl.cur(s_ve))
                    qchP = [qch, qch2]
                    kchP = [kch, kch2]
                    lds = {}
                    chkcons = {}
                    vcons = {}
                    lds[0] = (GD(qch[:], q_d[0], s_ld1),
                              GD(kch[:], k_d[0], s_ld2),
                              GD(vch[:], v_d[0], s_ld3))
                    if NCH > 1:
                        lds[1] = (GD(qch2[:], q_d[1], s_ldA),
                                  GD(kch2[:], k_d[1], s_spA), None)
                    for c in range(NCH):
                        par3 = c % 2
                        qb = qchP[par3]
                        kb = kchP[par3]
                        for buf, ldw in ((qb, lds[c][0]), (kb, lds[c][1])):
                            for gi in range(4):
                                MM(pbs[5][:, 0:512], 5, c_swap[:],
                                   buf[:][:, gi * 512:(gi + 1) * 512], True, True,
                                   waits=[ldw] if gi == 0 else ())
                                EVAC("vector",
                                     ("tt", stage2[:][:, gi * 512:(gi + 1) * 512]
                                      .rearrange("p (a t) -> p a t", a=4),
                                      pbs[5][:].rearrange("p (a t) -> p a t", a=4),
                                      bc_heads(c_rotB[:][:, c * 128:(c + 1) * 128], 4),
                                      OP.mult), 5)
                            swm = (s_mm, pl.cur(s_mm))
                            V(("tt", buf[:].rearrange("p (a t) -> p a t", a=NH),
                               buf[:].rearrange("p (a t) -> p a t", a=NH),
                               bc_heads(c_rotA[:][:, c * 128:(c + 1) * 128], NH),
                               OP.mult), waits=[swm])
                            VD()
                            V(("tt", buf[:], buf[:], stage2[:], OP.add))
                            VD()
                        rotv = (s_ve, pl.cur(s_ve))
                        if "qrot" in dbg:
                            GD(dbg["qrot"][c], qb[:], s_dbg, waits=[rotv])
                        if "krot" in dbg:
                            GD(dbg["krot"][c], kb[:], s_dbg, waits=[rotv])
                        V(("tt", qtch[:], qb[:], c_qdec[:], OP.mult),
                          waits=[rotv, (s_spB, pl.cur(s_spB)),
                                 chkcons.get(c - 1)])
                        V(("tt", ktch[:], kb[:], c_kdec[:], OP.mult))
                        tilv = (s_ve, pl.cur(s_ve))
                        GD(qt_d[c], qtch[:], s_spB, waits=[tilv])
                        for h in range(NH):
                            hsl = slice(h * 128, (h + 1) * 128)
                            bk = 5 if h % 2 == 0 else 3
                            bk2 = 7 if h % 2 == 0 else 6
                            TR(pbs[bk][:, 0:128], bk, ktch[:][:, hsl], c_identf[:],
                               waits=[tilv] if h == 0 else ())
                            EVAC("scalar", ("scopy", kttm[:][:, hsl],
                                            pbs[bk][:, 0:128]), bk,
                                 waits=[chkcons.get(c - 1)] if h == 0 else ())
                            TR(pbs[bk2][:, 0:128], bk2, vch[:][:, hsl], c_identf[:],
                               waits=[lds[c][2]] if h == 0 else ())
                            EVAC("vector", ("copy", vtm[:][:, hsl],
                                            pbs[bk2][:, 0:128]), bk2,
                                 waits=[chkcons.get(c - 1)] if h == 0 else ())
                        trv = (s_sc, pl.cur(s_sc))
                        trv2 = (s_ve, pl.cur(s_ve))
                        vcons[c] = (s_mm, pl.cur(s_mm))
                        if c + 1 < NCH:
                            ldv = GD(vch[:], v_d[c + 1], s_ld3,
                                     waits=[vcons[c]])
                            lds[c + 1] = (lds[c + 1][0], lds[c + 1][1], ldv) \
                                if c + 1 in lds else (None, None, ldv)
                        for h in range(NH):
                            hsl = slice(h * 128, (h + 1) * 128)
                            bs = 0 if h % 2 == 0 else 6
                            ba = 1 if h % 2 == 0 else 4
                            bd = 2 if h % 2 == 0 else 7
                            scmv = stage[:][:, (h % 2) * 128:(h % 2) * 128 + 128]
                            MM(pbs[bs][:, 0:128], bs, kb[:][:, hsl], qb[:][:, hsl],
                               True, True)
                            scv = EVAC("vector", ("tt", scmv, pbs[bs][:, 0:128],
                                                  c_dmaskT[:][:, hsl], OP.mult), bs,
                                       waits=[chkcons.get(c - 1)]
                                       if h == 0 else ())
                            MM(pbs[ba][:, 0:128], ba, scmv, vtm[:][:, hsl],
                               True, False,
                               waits=[scv] + ([trv, trv2, pzero] if h == 0 else []))
                            MM(pbs[ba][:, 0:128], ba, qtch[:][:, hsl], pkv[:][:, hsl],
                               False, True)
                            MM(pbs[bd][:, 0:128], bd, kttm[:][:, hsl], vtm[:][:, hsl],
                               True, True)
                            EVAC("scalar", ("scopy", atm_t(c)[:, hsl],
                                            pbs[ba][:, 0:128]), ba)
                            EVAC("vector", ("stt", pkv[:][:, hsl], pkv[:][:, hsl],
                                            r_C[h], pbs[bd][:, 0:128], OP.mult, OP.add), bd)
                        chkcons[c] = (s_mm, pl.cur(s_mm))
                        if c + 2 < NCH:
                            lq3 = GD(qchP[par3][:], q_d[c + 2],
                                     s_ld1 if par3 == 0 else s_ldA,
                                     waits=[chkcons[c]])
                            lk3 = GD(kchP[par3][:], k_d[c + 2],
                                     s_ld2 if par3 == 0 else s_spA,
                                     waits=[chkcons[c]])
                            old = lds.get(c + 2)
                            lds[c + 2] = (lq3, lk3, old[2] if old else None)
                        if c % 2 == 1:
                            pl.flush("B3")
                    pl.flush("B3")

                    gs = GD(cc_in, pkv[:], s_spB)
                    pl.emit("gpsimd", ("cc", [list(range(NCORES))], cc_in, cc_out),
                            [gs], [(s_cc, 1)])
                    pl.flush("B4")
                    if "sall" in dbg:
                        GD(dbg["sall"], cc_out, s_dbg, waits=[(s_cc, pl.cur(s_cc))])
                    for j in range(NCORES):
                        lj = GD(stage[:], cc_out[j * 128:(j + 1) * 128, :],
                                s_ld1, waits=[(s_cc, pl.cur(s_cc))] if j == 0 else ())
                        for h in range(NH):
                            hsl = slice(h * 128, (h + 1) * 128)
                            if j == 0:
                                V(("ts", pkv[:][:, hsl], stage[:][:, hsl],
                                   c_coef[:][:, 0 * NH + h: 0 * NH + h + 1], None,
                                   OP.mult, OP.bypass),
                                  waits=[lj] if h == 0 else (), inc=(h == NH - 1))
                            else:
                                V(("stt", pkv[:][:, hsl], stage[:][:, hsl],
                                   c_coef[:][:, j * NH + h: j * NH + h + 1],
                                   pkv[:][:, hsl], OP.mult, OP.add),
                                  waits=[lj] if h == 0 else (), inc=(h == NH - 1))
                        pl.defer("gpsimd", [(s_ve, pl.cur(s_ve))])
                    pl.emit("gpsimd",
                            ("cc", [list(range(NCORES))], cct_in, cct_out),
                            [(s_spD, pl.cur(s_spD)), (s_spE, pl.cur(s_spE))],
                            [(s_cc, 1)])
                    pl.flush("B4")
                    if "kvin" in dbg:
                        GD(dbg["kvin"], pkv[:], s_dbg)
                        FENCE((s_dbg, pl.cur(s_dbg)))
                        pl.flush("B4")

                    FENCE((s_spB, pl.cur(s_spB)))
                    pl.flush("B5")
                    qb5 = [qtch, qch]
                    lq5 = {}
                    cons5 = {}
                    mv = (s_ve, pl.cur(s_ve))
                    lq5[0] = GD(qb5[0][:], qt_d[0], s_ld1)
                    if NCH > 1:
                        lq5[1] = GD(qb5[1][:], qt_d[1], s_ld2)
                    for c in range(NCH):
                        for h in range(NH):
                            hsl = slice(h * 128, (h + 1) * 128)
                            bk = h % 4
                            MM(pbs[bk][:, 0:128], bk, qb5[c % 2][:][:, hsl],
                               pkv[:][:, hsl],
                               True, True, waits=[lq5[c], mv] if h == 0 else ())
                            EVAC("vector", ("tt", atm_t(c)[:, hsl], atm_t(c)[:, hsl],
                                            pbs[bk][:, 0:128], OP.add), bk)
                        cons5[c] = (s_mm, pl.cur(s_mm))
                        if c + 2 < NCH:
                            lq5[c + 2] = GD(qb5[c % 2][:], qt_d[c + 2],
                                            s_ld1 if c % 2 == 0 else s_ld2,
                                            waits=[cons5[c]])
                        if c < NCH - 1:
                            for h in range(NH):
                                hsl = slice(h * 128, (h + 1) * 128)
                                V(("ts", pkv[:][:, hsl], pkv[:][:, hsl], r_C[h],
                                   None, OP.mult, OP.bypass),
                                  waits=[cons5[c]] if h == 0 else (),
                                  inc=(h == NH - 1))
                            mv = (s_ve, pl.cur(s_ve))
                        if c % 4 == 3:
                            pl.flush("B5")
                    if "attn" in dbg:
                        for c in range(NCH):
                            GD(dbg["attn"][c], atm_t(c), s_dbg)
                        pl.flush("B5")

                FENCE_ALL()
                chunk_ctx.close()
                b6_ctx = ExitStack()
                rmsw_b = b6_ctx.enter_context(nc.sbuf_tensor("rmsw_b", [128, D], F32))
                stgB6 = b6_ctx.enter_context(nc.sbuf_tensor("stgB6", [128, D], F32))
                stg2A6 = b6_ctx.enter_context(nc.sbuf_tensor("stg2A6", [128, D], F32))
                stg2B6 = b6_ctx.enter_context(nc.sbuf_tensor("stg2B6", [128, D], F32))
                st6 = [stage, stgB6]
                st26 = [stg2A6, stg2B6]
                trd6 = {0: None, 1: None}
                spl6 = {0: None, 1: None}
                bcast_row(c_rmsw, D, 4, rmsw_b, waits=[cin0])
                for c in range(NCH):
                    par = c % 2
                    stg = st6[par]
                    stg2 = st26[par]
                    cb = par * 4
                    sq = SC(("act", stg[:], atm_t(c), ACT.Square, 0.0, 1.0),
                            waits=[trd6[par]])
                    V(("red", colbuf[:][:, cb:cb + 1], stg[:], OP.add, False),
                      waits=[sq])
                    VD()
                    V(("ts", colbuf[:][:, cb + 1:cb + 2], colbuf[:][:, cb:cb + 1],
                       1.0 / D, 1e-5, OP.mult, OP.add))
                    sq2 = SC(("act", colbuf[:][:, cb + 2:cb + 3],
                              colbuf[:][:, cb + 1:cb + 2],
                              ACT.Sqrt, 0.0, 1.0), waits=[(s_ve, pl.cur(s_ve))])
                    V(("recip", colbuf[:][:, cb + 3:cb + 4],
                       colbuf[:][:, cb + 2:cb + 3]), waits=[sq2])
                    VD()
                    V(("ts", stg[:], atm_t(c), colbuf[:][:, cb + 3:cb + 4], None,
                       OP.mult, OP.bypass))
                    VD()
                    V(("tt", stg[:], stg[:], rmsw_b[:], OP.mult))
                    VD()
                    anv = (s_ve, pl.cur(s_ve))
                    if "attn_n" in dbg:
                        GD(dbg["attn_n"][c], stg[:], s_dbg, waits=[anv])
                    for dc in range(NDC):
                        bk = dc % 4
                        TR(pbs[bk][:, 0:128], bk,
                           stg[:][:, dc * 128:(dc + 1) * 128], c_identf[:],
                           waits=[anv] if dc == 0 else ())
                        eng6 = "vector" if dc % 2 == 0 else "scalar"
                        EVAC(eng6, (("copy" if dc % 2 == 0 else "scopy"),
                                    stg2[:][:, dc * 128:(dc + 1) * 128],
                                    pbs[bk][:, 0:128]), bk,
                             waits=[spl6[par]] if (dc < 2 and spl6[par]) else ())
                    trd6[par] = (s_mm, pl.cur(s_mm))
                    gv = GD(afm_d[:, :, c * 128:(c + 1) * 128]
                            .rearrange("a p t -> p a t"),
                            stg2[:].rearrange("p (a t) -> p a t", a=NDC),
                            s_spC if par == 0 else s_spB,
                            waits=[(s_ve, pl.cur(s_ve)),
                                   (s_sc, pl.cur(s_sc))])
                    spl6[par] = gv
                    pl.defer("gpsimd", [(s_dbg, pl.cur(s_dbg))])
                    if c % 4 == 3:
                        pl.flush("B6")
                FENCE_ALL()
                b6_ctx.close()

            FENCE_ALL()
            atm_ctx.close()
        attn_ctx.close()
        rows_ctx = ExitStack()
        rows = rows_ctx.enter_context(nc.sbuf_tensor("rows", [1, 6 * 1024], F32))
        b7_ctx = ExitStack()
        ahi7 = b7_ctx.enter_context(nc.sbuf_tensor("ahi7", [128, NDC * TPC], BF16))
        alo7 = b7_ctx.enter_context(nc.sbuf_tensor("alo7", [128, NDC * TPC], BF16))
        stg7 = [b7_ctx.enter_context(nc.sbuf_tensor(f"stg7{i}", [128, D], F32))
                for i in range(2)]
        stq7 = [b7_ctx.enter_context(nc.sbuf_tensor(f"stq7{i}", [128, D], F32))
                for i in range(2)]
        if True:
            b7done = {0: None, 1: None}
            b7sp = {0: None, 1: None}
            FENCE((s_spC, pl.cur(s_spC)))
            pl.flush("B7")
            afm_ctx2 = ExitStack()
            afm = afm_ctx2.enter_context(nc.sbuf_tensor("afm", [128, NDC * TPC], F32))
            afl7 = {}
            spld7 = {}
            afl7[0] = SCY(afm[:][:, 0:TPC], afm_d[0], semx=s_w5)
            afl7[1] = SCY(afm[:][:, TPC:2 * TPC], afm_d[1], semx=s_w6)
            V(("memset", rows[:][:, 0:2 * 1024], 0.0))
            rz = (s_ve, pl.cur(s_ve))
            for dc in range(NDC):
                V(("copy", ahi7[:][:, dc * TPC:(dc + 1) * TPC],
                   afm[:][:, dc * TPC:(dc + 1) * TPC]),
                  waits=[afl7[dc]])
                VD()
                V(("tt", alo7[:][:, dc * TPC:(dc + 1) * TPC],
                   afm[:][:, dc * TPC:(dc + 1) * TPC],
                   ahi7[:][:, dc * TPC:(dc + 1) * TPC], OP.subtract))
                VD()
                spld7[dc] = (s_ve, pl.cur(s_ve))
                if dc + 2 < NDC:
                    afl7[dc + 2] = SCY(afm[:][:, (dc + 2) * TPC:(dc + 3) * TPC],
                                       afm_d[dc + 2],
                                       semx=s_w5 if dc % 2 == 0 else s_w6,
                                       waits=[spld7[dc]])
            afsv = (s_ve, pl.cur(s_ve))
            pl.flush("B7s")
            FENCE_ALL()
            afm_ctx2.close()
            w7_ctx = ExitStack()
            w7f = [w7_ctx.enter_context(nc.sbuf_tensor(f"w7f{i}", [128, NDC * 256], F32))
                   for i in range(2)]
            w7h = [w7_ctx.enter_context(nc.sbuf_tensor(f"w7h{i}", [128, NDC * 256], BF16))
                   for i in range(2)]
            w7l = [w7_ctx.enter_context(nc.sbuf_tensor(f"w7l{i}", [128, NDC * 256], BF16))
                   for i in range(2)]
            lw7 = {}
            spl7 = {}
            mmc7 = {}

            def b7_load(j):
                par = j % 2
                lw7[j] = SY(w7f[par][:].rearrange("p (c e) -> p c e", c=NDC),
                            ap["wo"][:, j * 256:(j + 1) * 256]
                            .rearrange("(c p) e -> p c e", p=128),
                            semx=s_w if par == 0 else s_w2,
                            waits=[spl7.get(j - 2)])

            def b7_split(j):
                par = j % 2
                V(("copy", w7h[par][:], w7f[par][:]),
                  waits=[lw7[j], mmc7.get(j - 2)])
                VD()
                V(("tt", w7l[par][:], w7f[par][:], w7h[par][:], OP.subtract))
                VD()
                spl7[j] = (s_ve, pl.cur(s_ve))

            pend_stat = [None]

            def flush_stat():
                if pend_stat[0]:
                    pend_stat[0]()
                    pend_stat[0] = None

            b7_load(0)
            b7_load(1)
            b7_split(0)
            for j in range(8):
                par8 = j % 2
                if j + 2 < 8:
                    b7_load(j + 2)
                if j + 1 < 8:
                    b7_split(j + 1)
                for ebl2 in range(2):
                    eb = j * 2 + ebl2
                    par = eb % 2
                    stg = stg7[par]
                    stq = stq7[par]
                    for sl in range(NSL):
                        bk = par * 2 + sl
                        for dc in range(NDC):
                            wth = w7h[par8][:][:, dc * 256 + ebl2 * 128:
                                               dc * 256 + (ebl2 + 1) * 128]
                            wtl = w7l[par8][:][:, dc * 256 + ebl2 * 128:
                                               dc * 256 + (ebl2 + 1) * 128]
                            ah = ahi7[:][:, dc * TPC + sl * SL:
                                         dc * TPC + (sl + 1) * SL]
                            al = alo7[:][:, dc * TPC + sl * SL:
                                         dc * TPC + (sl + 1) * SL]
                            MM(pbs[bk][:, 0:SL], bk, wth, ah, dc == 0, False,
                               waits=[spl7[j], afsv] if dc == 0 else ())
                            MM(pbs[bk][:, 0:SL], bk, wth, al, False, False)
                            MM(pbs[bk][:, 0:SL], bk, wtl, ah, False,
                               dc == NDC - 1)
                    flush_stat()
                    lx = GD(stq[:][:, 0:TPC], x_fm_d[eb], s_ld2,
                            waits=[b7done[par]] if b7done[par] else ())
                    for sl in range(NSL):
                        bk = par * 2 + sl
                        EVAC("vector",
                             ("tt", stg[:][:, sl * SL:(sl + 1) * SL],
                              stq[:][:, sl * SL:(sl + 1) * SL], pbs[bk][:, 0:SL],
                              OP.add), bk,
                             waits=[lx] + ([b7sp[par], b7done[par]]
                                           if (sl == 0 and b7sp[par]) else []))
                    yev = (s_ve, pl.cur(s_ve))
                    gv = GD(y1_d[eb], stg[:][:, 0:TPC],
                            s_spD if par == 0 else s_spF, waits=[yev])
                    if "y1" in dbg:
                        GD(dbg["y1"][eb], stg[:][:, 0:TPC], s_dbg, waits=[yev])
                    sq = SC(("act", stq[:][:, 0:TPC], stg[:][:, 0:TPC],
                             ACT.Square, 0.0, 1.0), waits=[yev])

                    def mk_stat(eb=eb, par=par, yev=yev, sq=sq,
                                stg=stg, stq=stq):
                        def f():
                            for sl in range(NSL):
                                MM(pbs[4 + sl][0:1, 0:SL], 4 + sl, c_onesc[:],
                                   stg[:][:, sl * SL:(sl + 1) * SL],
                                   eb == 0, eb == 15, waits=[yev],
                                   mark=True)
                                MM(pbs[6 + sl][0:1, 0:SL], 6 + sl, c_onesc[:],
                                   stq[:][:, sl * SL:(sl + 1) * SL],
                                   eb == 0, eb == 15, waits=[sq],
                                   mark=True)
                            b7done[par] = (s_mm, pl.cur(s_mm))
                        return f
                    pend_stat[0] = mk_stat()
                    b7sp[par] = gv
                    pl.defer("gpsimd", [(s_dbg, pl.cur(s_dbg))])
                mmc7[j] = (s_mm, pl.cur(s_mm))
                if j % 4 == 3:
                    pl.flush("B7")
            flush_stat()
            for sl in range(NSL):
                EVAC("vector", ("copy", rows[:][:, sl * SL:(sl + 1) * SL],
                                pbs[4 + sl][0:1, 0:SL]), 4 + sl)
                EVAC("vector",
                     ("copy", rows[:][:, 1024 + sl * SL:1024 + (sl + 1) * SL],
                      pbs[6 + sl][0:1, 0:SL]), 6 + sl)
            pl.flush("B7")
            FENCE_ALL()
            w7_ctx.close()
        FENCE_ALL()
        b7_ctx.close()
        V(("ts", rows[:][:, 2 * 1024:2 * 1024 + TPC], rows[:][:, 0:TPC],
           1.0 / D, None, OP.mult, OP.bypass))
        V(("ts", rows[:][:, 3 * 1024:3 * 1024 + TPC], rows[:][:, 1024:1024 + TPC],
           1.0 / D, None, OP.mult, OP.bypass))
        VD()
        V(("tt", rows[:][:, 4 * 1024:4 * 1024 + TPC],
           rows[:][:, 2 * 1024:2 * 1024 + TPC],
           rows[:][:, 2 * 1024:2 * 1024 + TPC], OP.mult))
        VD()
        V(("tt", rows[:][:, 3 * 1024:3 * 1024 + TPC],
           rows[:][:, 3 * 1024:3 * 1024 + TPC],
           rows[:][:, 4 * 1024:4 * 1024 + TPC], OP.subtract))
        VD()
        V(("ts", rows[:][:, 3 * 1024:3 * 1024 + TPC],
           rows[:][:, 3 * 1024:3 * 1024 + TPC], 1.0, 1e-5, OP.mult, OP.add))
        sqv = SC(("act", rows[:][:, 4 * 1024:4 * 1024 + TPC],
                  rows[:][:, 3 * 1024:3 * 1024 + TPC], ACT.Sqrt, 0.0, 1.0),
                 waits=[(s_ve, pl.cur(s_ve))])
        V(("recip", rows[:][:, 3 * 1024:3 * 1024 + TPC],
           rows[:][:, 4 * 1024:4 * 1024 + TPC]), waits=[sqv])
        VD()
        V(("tt", rows[:][:, 4 * 1024:4 * 1024 + TPC],
           rows[:][:, 2 * 1024:2 * 1024 + TPC],
           rows[:][:, 3 * 1024:3 * 1024 + TPC], OP.mult))
        VD()
        V(("ts", rows[:][:, 4 * 1024:4 * 1024 + TPC],
           rows[:][:, 4 * 1024:4 * 1024 + TPC], -1.0, None, OP.mult, OP.bypass))
        lnv = (s_ve, pl.cur(s_ve))
        GD(ln1_d[:, 0:TPC], rows[:][:, 3 * 1024:3 * 1024 + TPC], s_spD,
           waits=[lnv])
        GD(ln1_d[:, 1024:1024 + TPC], rows[:][:, 4 * 1024:4 * 1024 + TPC], s_spD)
        if "ln1" in dbg:
            GD(dbg["ln1"][:, 0:1024], rows[:][:, 3 * 1024:4 * 1024], s_dbg)
            GD(dbg["ln1"][:, 1024:2048], rows[:][:, 4 * 1024:5 * 1024], s_dbg)
        FENCE_ALL()
        rows_ctx.close()
        pl.flush("B7")

        b9_ctx = ExitStack()
        ghi = b9_ctx.enter_context(nc.sbuf_tensor("ghi", [128, NHC * TQ], F32R))
        glo = b9_ctx.enter_context(nc.sbuf_tensor("glo", [128, NHC * TQ], F32R))
        x2hi = b9_ctx.enter_context(nc.sbuf_tensor("x2hi", [128, NDC * TQ], F32R))
        x2lo = b9_ctx.enter_context(nc.sbuf_tensor("x2lo", [128, NDC * TQ], F32R))
        wsA = b9_ctx.enter_context(nc.sbuf_tensor("wsA", [128, HIDP], F32R))
        wsB = b9_ctx.enter_context(nc.sbuf_tensor("wsB", [128, HIDP], F32R))
        yqb = b9_ctx.enter_context(nc.sbuf_tensor("yqb", [128, NDC * TQ], F32))
        rowsq = b9_ctx.enter_context(nc.sbuf_tensor("rowsq", [1, 5 * TQ], F32))
        y1pc = [b9_ctx.enter_context(nc.sbuf_tensor(f"y1pc{i}", [128, TQ], F32))
                for i in range(2)]
        bca2 = b9_ctx.enter_context(nc.sbuf_tensor("bca2", [128, TQ], F32))
        bcb2 = b9_ctx.enter_context(nc.sbuf_tensor("bcb2", [128, TQ], F32))
        x2sc = b9_ctx.enter_context(nc.sbuf_tensor("x2sc", [128, TQ], F32))
        if True:
            wbufs = [wsA, wsB]
            csem = [s_cast, s_cast2]
            FENCE((s_spD, pl.cur(s_spD)), (s_spE, pl.cur(s_spE)),
                  (s_spF, pl.cur(s_spF)))
            pl.flush("B9h")
            pl.defer("gpsimd", [(s_cc, pl.cur(s_cc))])
            consW = {0: None, 1: None}
            x2con = None
            ghcon = None
            yqcon = None
            rqcon = None
            x2v = None
            bcaP = [bca, bca2]
            bcbP = [bcb, bcb2]
            lyq = None

            def t1v(hb):
                return cct_out[hb % NCORES][(hb // NCORES) * NDC:
                                            (hb // NCORES + 1) * NDC]

            def t3v(hb):
                return cct_out[hb % NCORES][S1 * NDC + (hb // NCORES) * NDC:
                                            S1 * NDC + (hb // NCORES + 1) * NDC]

            def t2v(db):
                b0 = 2 * S1 * NDC + (db // NCORES) * NHC
                return cct_out[db % NCORES][b0:b0 + NHC]

            def load_g_strips(hb, par):
                pl.emit("gpsimd",
                        ("dma", wbufs[par][:][:, 0:NDC * 128]
                         .rearrange("p (a t) -> p a t", a=NDC),
                         t1v(hb).rearrange("a p t -> p a t")),
                        [consW[par]] if consW[par] else [], [(csem[par], 16)])
                pl.emit("gpsimd",
                        ("dma", wbufs[par][:][:, NDC * 128:2 * NDC * 128]
                         .rearrange("p (a t) -> p a t", a=NDC),
                         t3v(hb).rearrange("a p t -> p a t")),
                        [], [(csem[par], 16)])
                return (csem[par], pl.cur(csem[par]))

            def load_y_strip(db, par):
                pl.emit("gpsimd",
                        ("dma", wbufs[par][:][:, 0:NHC * 128]
                         .rearrange("p (a t) -> p a t", a=NHC),
                         t2v(db).rearrange("a p t -> p a t")),
                        [consW[par]] if consW[par] else [], [(csem[par], 16)])
                return (csem[par], pl.cur(csem[par]))

            def ln1_rows(tq):
                tq0 = tq * TQ
                pl.emit("gpsimd", ("dma", rowsq[:, 0:TQ], ln1_d[:, tq0:tq0 + TQ]),
                        [rqcon], [(s_ld3, 16)])
                pl.emit("gpsimd", ("dma", rowsq[:, TQ:2 * TQ],
                                   ln1_d[:, 1024 + tq0:1024 + tq0 + TQ]),
                        [], [(s_ld3, 16)])
                lr = (s_ld3, pl.cur(s_ld3))
                bcast_row(rowsq[:, 0:TQ], TQ, 6, bcaP[tq % 2], waits=[lr])
                bcast_row(rowsq[:, TQ:2 * TQ], TQ, 6, bcbP[tq % 2])
                return (s_mm, pl.cur(s_mm)), (s_ve, pl.cur(s_ve))

            def x2_prep(tq, xwar):
                tq0 = tq * TQ
                ldp = {}
                pcon = {0: None, 1: None}
                pwar = list(x2v) if x2v else []
                ldp[0] = GD(y1pc[0][:], y1_d[:, :, tq0:tq0 + TQ]
                            .rearrange("a p t -> p a t")[:, 0], s_ld1,
                            waits=pwar)
                ldp[1] = GD(y1pc[1][:], y1_d[:, :, tq0:tq0 + TQ]
                            .rearrange("a p t -> p a t")[:, 1], s_ld2,
                            waits=pwar)
                for dc in range(NDC):
                    par = dc % 2
                    if par == 0:
                        V(("tt", gtmp2[:], y1pc[0][:], bcaP[tq % 2][:], OP.mult),
                          waits=[ldp[dc]] + ([xwar, bcv2] if dc == 0 else []))
                        VD()
                        V(("tt", gtmp2[:], gtmp2[:], bcbP[tq % 2][:], OP.add))
                        VD()
                        V(("copy", x2hi[:][:, dc * TQ:(dc + 1) * TQ], gtmp2[:]))
                        VD()
                        V(("tt", x2lo[:][:, dc * TQ:(dc + 1) * TQ], gtmp2[:],
                           x2hi[:][:, dc * TQ:(dc + 1) * TQ].bitcast(F32),
                           OP.subtract))
                        VD()
                        pcon[0] = (s_ve, pl.cur(s_ve))
                    else:
                        G(("tt", x2sc[:], y1pc[1][:], bcaP[tq % 2][:], OP.mult),
                          waits=[ldp[dc]] + ([xwar, bcv2] if dc == 1 else []))
                        GPD()
                        G(("tt", x2sc[:], x2sc[:], bcbP[tq % 2][:], OP.add))
                        GPD()
                        G(("copy", x2hi[:][:, dc * TQ:(dc + 1) * TQ], x2sc[:]))
                        GPD()
                        G(("tt", x2lo[:][:, dc * TQ:(dc + 1) * TQ], x2sc[:],
                           x2hi[:][:, dc * TQ:(dc + 1) * TQ].bitcast(F32),
                           OP.subtract))
                        GPD()
                        pcon[1] = (s_gp, pl.cur(s_gp))
                    if dc + 2 < NDC:
                        ldp[dc + 2] = GD(y1pc[par][:],
                                         y1_d[:, :, tq0:tq0 + TQ]
                                         .rearrange("a p t -> p a t")[:, dc + 2],
                                         s_ld1 if par == 0 else s_ld2,
                                         waits=[pcon[par]])
                return [(s_ve, pl.cur(s_ve)), (s_gp, pl.cur(s_gp))]

            rqcon, bcv2 = ln1_rows(0)
            x2v = x2_prep(0, None)
            lyq = GD(yqb[:].rearrange("p (a t) -> p a t", a=NDC),
                     y1_d[:, :, 0:TQ].rearrange("a p t -> p a t"), s_spF)
            lcur = load_g_strips(0, 0)
            pl.flush("B9p")
            for tq in range(NTQ):
                tq0 = tq * TQ
                for hb in range(NHC):
                    par = hb % 2
                    if hb + 1 < NHC:
                        lnext = load_g_strips(hb + 1, 1 - par)
                    buf = wbufs[par][:]
                    b1, b3 = hb % 2, 2 + hb % 2
                    for dc in range(NDC):
                        MM(pbs[b1][:, 0:TQ], b1,
                           buf[:, dc * 128:(dc + 1) * 128],
                           x2hi[:][:, dc * TQ:(dc + 1) * TQ],
                           dc == 0, False,
                           waits=([lcur] + x2v) if dc == 0 else ())
                        MM(pbs[b1][:, 0:TQ], b1,
                           buf[:, dc * 128:(dc + 1) * 128],
                           x2lo[:][:, dc * TQ:(dc + 1) * TQ],
                           False, dc == NDC - 1)
                        MM(pbs[b3][:, 0:TQ], b3,
                           buf[:, NDC * 128 + dc * 128:NDC * 128 + (dc + 1) * 128],
                           x2hi[:][:, dc * TQ:(dc + 1) * TQ],
                           dc == 0, False)
                        MM(pbs[b3][:, 0:TQ], b3,
                           buf[:, NDC * 128 + dc * 128:NDC * 128 + (dc + 1) * 128],
                           x2lo[:][:, dc * TQ:(dc + 1) * TQ],
                           False, dc == NDC - 1,
                           mark=(dc == NDC - 1 and hb == NHC - 1))
                    consW[par] = (s_mm, pl.cur(s_mm))
                    if hb == NHC - 1:
                        x2con = (s_mm, pl.cur(s_mm))
                    if hb + 1 < NHC:
                        lcur = lnext
                    sv = EVAC("scalar", ("act", gtmp[:], pbs[b1][:, 0:TQ], ACT.Silu,
                                         0.0, scol[:][:, 0:1]), b1,
                              waits=[(s_ve, pl.cur(s_ve))])
                    gv2 = EVAC("vector", ("stt", gtmp2[:], pbs[b3][:, 0:TQ],
                                          scol[:][:, 1:2], gtmp[:], OP.mult,
                                          OP.mult), b3, waits=[sv])
                    VD()
                    V(("copy", ghi[:][:, hb * TQ:(hb + 1) * TQ], gtmp2[:]),
                      waits=[ghcon] if hb == 0 else ())
                    VD()
                    V(("tt", glo[:][:, hb * TQ:(hb + 1) * TQ], gtmp2[:],
                       ghi[:][:, hb * TQ:(hb + 1) * TQ].bitcast(F32), OP.subtract))
                    VD()
                    if hb == 20:
                        pl.flush(f"B9G_{tq}")
                gfv = (s_ve, pl.cur(s_ve))
                if "gq" in dbg:
                    for hb in range(NHC):
                        V(("copy", gtmp[:], ghi[:][:, hb * TQ:(hb + 1) * TQ]
                           .bitcast(F32)), waits=[gfv] if hb == 0 else ())
                        VD()
                        V(("tt", gtmp[:], gtmp[:],
                           glo[:][:, hb * TQ:(hb + 1) * TQ].bitcast(F32), OP.add))
                        GD(dbg["gq"][tq][hb], gtmp[:],
                           s_dbg, waits=[(s_ve, pl.cur(s_ve))])
                        pl.defer("vector", [(s_dbg, pl.cur(s_dbg))])
                    pl.defer("scalar", [(s_dbg, pl.cur(s_dbg))])
                if tq + 1 < NTQ:
                    rqcon, bcv2 = ln1_rows(tq + 1)

                pend9 = [None]

                def flush9():
                    if pend9[0]:
                        pend9[0]()
                        pend9[0] = None
                ycur = load_y_strip(0, 0)
                for db in range(NDC):
                    par = db % 2
                    bank = (db // 2) % 4
                    sub = db % 2
                    if db + 1 < NDC:
                        ynext = load_y_strip(db + 1, 1 - par)
                    buf = wbufs[par][:]
                    for hc in range(NHC):
                        MM(pbs[bank][:, sub * TQ:(sub + 1) * TQ], bank,
                           buf[:, hc * 128:(hc + 1) * 128],
                           ghi[:][:, hc * TQ:(hc + 1) * TQ],
                           (sub == 0 and hc == 0), False,
                           waits=[ycur, gfv] if hc == 0 else ())
                        MM(pbs[bank][:, sub * TQ:(sub + 1) * TQ], bank,
                           buf[:, hc * 128:(hc + 1) * 128],
                           glo[:][:, hc * TQ:(hc + 1) * TQ],
                           False, (sub == 1 and hc == NHC - 1),
                           mark=(sub == 0 and hc == NHC - 1))
                    consW[par] = (s_mm, pl.cur(s_mm))
                    if db == NDC - 1:
                        ghcon = (s_mm, pl.cur(s_mm))
                    if db + 1 < NDC:
                        ycur = ynext
                    if sub == 1:
                        flush9()
                        for s2 in range(2):
                            db2 = (db // 2) * 2 + s2
                            V(("tt", gtmp2[:], yqb[:][:, db2 * TQ:(db2 + 1) * TQ],
                               bcaP[tq % 2][:], OP.mult),
                              waits=[lyq] if (bank == 0 and s2 == 0) else ())
                            VD()
                            V(("tt", gtmp2[:], gtmp2[:], bcbP[tq % 2][:], OP.add))
                            VD()
                            EVAC("vector",
                                 ("stt", yqb[:][:, db2 * TQ:(db2 + 1) * TQ],
                                  pbs[bank][:, s2 * TQ:(s2 + 1) * TQ],
                                  scol[:][:, 2:3], gtmp2[:],
                                  OP.mult, OP.add), bank)
                            VD()
                        yev9 = (s_ve, pl.cur(s_ve))

                        def mk9(bank=bank, yev9=yev9, pb=(db // 2) * 2):
                            def f():
                                for s2 in range(2):
                                    db2 = pb + s2
                                    sq9 = SC(("act", gtmp[:],
                                              yqb[:][:, db2 * TQ:(db2 + 1) * TQ],
                                              ACT.Square, 0.0, 1.0), waits=[yev9])
                                    MM(pbs[6][0:1, 0:TQ], 6, c_onesc[:],
                                       yqb[:][:, db2 * TQ:(db2 + 1) * TQ],
                                       db2 == 0, False, waits=[yev9],
                                       mark=True)
                                    MM(pbs[6][0:1, TQ:2 * TQ], 6, c_onesc[:],
                                       gtmp[:], False, db2 == NDC - 1,
                                       waits=[sq9], mark=True)
                                    pl.defer("scalar",
                                             [(s_mm, pl.cur(s_mm))])
                            return f
                        pend9[0] = mk9()
                    if db == 9 and tq + 1 < NTQ:
                        x2v = x2_prep(tq + 1, x2con)
                flush9()
                RM, RV, RT = 2 * TQ, 3 * TQ, 4 * TQ
                EVAC("vector", ("ts", rowsq[:, RM:RM + TQ], pbs[6][0:1, 0:TQ],
                                1.0 / D, None, OP.mult, OP.bypass), 6)
                V(("ts", rowsq[:, RV:RV + TQ], pbs[6][0:1, TQ:2 * TQ],
                   1.0 / D, None, OP.mult, OP.bypass))
                VD()
                V(("tt", rowsq[:, RT:RT + TQ], rowsq[:, RM:RM + TQ],
                   rowsq[:, RM:RM + TQ], OP.mult))
                VD()
                V(("tt", rowsq[:, RV:RV + TQ], rowsq[:, RV:RV + TQ],
                   rowsq[:, RT:RT + TQ], OP.subtract))
                VD()
                V(("ts", rowsq[:, RV:RV + TQ], rowsq[:, RV:RV + TQ],
                   1.0, 1e-5, OP.mult, OP.add))
                sq3 = SC(("act", rowsq[:, RT:RT + TQ], rowsq[:, RV:RV + TQ],
                          ACT.Sqrt, 0.0, 1.0), waits=[(s_ve, pl.cur(s_ve))])
                V(("recip", rowsq[:, RV:RV + TQ], rowsq[:, RT:RT + TQ]),
                  waits=[sq3])
                VD()
                V(("tt", rowsq[:, RT:RT + TQ], rowsq[:, RM:RM + TQ],
                   rowsq[:, RV:RV + TQ], OP.mult))
                VD()
                V(("ts", rowsq[:, RT:RT + TQ], rowsq[:, RT:RT + TQ],
                   -1.0, None, OP.mult, OP.bypass))
                bkv = (s_ve, pl.cur(s_ve))
                for ttile in range(NTT):
                    MM(pbs[4 + ttile % 2][:, 0:128], 4 + ttile % 2,
                       rowsq[0:1, RV + ttile * 128:RV + (ttile + 1) * 128],
                       c_identf[0:1, 0:128], True, True,
                       waits=[bkv] if ttile == 0 else ())
                    EVAC("vector", ("copy", colbuf[:][:, 8 + ttile:9 + ttile],
                                    pbs[4 + ttile % 2][:, 0:1]), 4 + ttile % 2)
                    MM(pbs[4 + ttile % 2][:, 0:128], 4 + ttile % 2,
                       rowsq[0:1, RT + ttile * 128:RT + (ttile + 1) * 128],
                       c_identf[0:1, 0:128], True, True)
                    EVAC("vector", ("copy", colbuf[:][:, 12 + ttile:13 + ttile],
                                    pbs[4 + ttile % 2][:, 0:1]), 4 + ttile % 2)
                for ttile in range(NTT):
                    for db in range(NDC):
                        bko = 4 + db % 4
                        TR(pbs[bko][:, 0:128], bko,
                           yqb[:][:, db * TQ + ttile * 128: db * TQ + (ttile + 1) * 128],
                           c_identf[:])
                        EVAC("vector",
                             ("ts", stage[:][:, db * 128:(db + 1) * 128],
                              pbs[bko][:, 0:128],
                              colbuf[:][:, 8 + ttile:9 + ttile],
                              colbuf[:][:, 12 + ttile:13 + ttile],
                              OP.mult, OP.add), bko)
                    ov = (s_ve, pl.cur(s_ve))
                    GD(out_t[tq0 + ttile * 128: tq0 + (ttile + 1) * 128, :],
                       stage[:], s_outd, waits=[ov])
                    pl.defer("vector", [(s_outd, pl.cur(s_outd))])
                yqcon = (s_mm, pl.cur(s_mm))
                if tq + 1 < NTQ:
                    lyq = GD(yqb[:].rearrange("p (a t) -> p a t", a=NDC),
                             y1_d[:, :, tq0 + TQ:tq0 + 2 * TQ]
                             .rearrange("a p t -> p a t"), s_spF,
                             waits=[yqcon])
                    lcur = load_g_strips(0, 0)

        FENCE_ALL()
        b9_ctx.close()
        pl.emit("gpsimd", ("nop",), [(s_outd, pl.cur(s_outd)), (s_dbg, pl.cur(s_dbg)), (s_spD, pl.cur(s_spD)),
                                     (s_cast, pl.cur(s_cast))], [])
        pl.flush(f"B9L_{tq}")

    return nc



def make_inputs(x_full, Wq, Wk, Wv, Wo, rms_w, w1, w3, w2, NCH=8, NCORES=8):
    TPC = NCH * CH
    cdc = _decay_constants()
    dmaskT = cdc["dmask"].transpose(0, 2, 1).transpose(1, 0, 2) \
        .reshape(128, NH * CH).copy().astype(np.float32)
    qdec = np.broadcast_to(cdc["q_decay"][:, None, :], (NH, 128, CH)) \
        .transpose(1, 0, 2).reshape(128, NH * CH).copy().astype(np.float32)
    kdec = np.broadcast_to(cdc["k_decay"][:, None, :], (NH, 128, CH)) \
        .transpose(1, 0, 2).reshape(128, NH * CH).copy().astype(np.float32)
    swap = np.zeros((128, 128), dtype=np.float32)
    for dd in range(128):
        swap[(dd + 64) % 128, dd] = 1.0
    r64 = cdc["r"].astype(np.float64)
    S1 = -(-NHC // NCORES)
    S2 = -(-NDC // NCORES)

    def rowshard(w, nslot, core):
        out = np.zeros((nslot * 128, w.shape[1]), dtype=np.float32)
        for sl_ in range(nslot):
            r0 = (sl_ * NCORES + core) * 128
            r1 = min(r0 + 128, w.shape[0])
            if r0 < w.shape[0]:
                out[sl_ * 128:sl_ * 128 + (r1 - r0)] = w[r0:r1]
        return out
    ins = []
    for core in range(NCORES):
        cos, sin = _rotary_tables(core * TPC, TPC)
        rotA = np.concatenate([cos.T, cos.T], axis=0).astype(np.float32)
        rotB = np.concatenate([sin.T, -sin.T], axis=0).astype(np.float32)
        coef = np.zeros((NCORES, NH), dtype=np.float32)
        for j in range(NCORES):
            if j < core:
                coef[j] = (r64 ** (128.0 * 8.0 * (core - 1 - j))).astype(np.float32)
        coef_b = np.broadcast_to(coef.reshape(1, -1),
                                 (128, NCORES * NH)).copy().astype(np.float32)
        d = dict(
            x=np.ascontiguousarray(x_full[core * TPC:(core + 1) * TPC]),
            wq=Wq, wk=Wk, wv=Wv, wo=Wo,
            rmsw_row=np.ascontiguousarray(rms_w.reshape(1, D)),
            w1s=rowshard(w1, S1, core), w3s=rowshard(w3, S1, core),
            w2s=rowshard(w2, S2, core),
            identf=np.eye(128, dtype=np.float32),
            identb=np.eye(128, dtype=ml_dtypes.bfloat16),
            swap64=swap,
            rotA=rotA, rotB=rotB,
            dmaskT=dmaskT, qdec=qdec, kdec=kdec,
            coef=coef_b,
            onescol=np.ones((128, 1), dtype=np.float32),
            onesrow=np.ones((1, 128), dtype=np.float32),
        )
        ins.append(d)
    return ins


_NC_CACHE = {}


def kernel(L0_attn_x, L0_attn_Wq, L0_attn_Wk, L0_attn_Wv, L0_attn_Wo,
           L0_attn_rms_w, L0_ffn_w1, L0_ffn_w3, L0_ffn_w2):
    from concourse.bass_utils import run_bass_kernel_spmd
    f32 = lambda a: np.ascontiguousarray(np.asarray(a, dtype=np.float32))
    x = f32(L0_attn_x).reshape(8192, D)
    ins = make_inputs(x, f32(L0_attn_Wq), f32(L0_attn_Wk), f32(L0_attn_Wv),
                      f32(L0_attn_Wo), f32(L0_attn_rms_w), f32(L0_ffn_w1),
                      f32(L0_ffn_w3), f32(L0_ffn_w2), NCH=8, NCORES=8)
    if "nc" not in _NC_CACHE:
        _NC_CACHE["nc"] = build(NCH=8, NCORES=8)
    res = run_bass_kernel_spmd(_NC_CACHE["nc"], ins, core_ids=list(range(8)))
    out = np.concatenate([res.results[c]["out"] for c in range(8)], axis=0)
    return out.reshape(1, 8192, D)


# version 16
# speedup vs baseline: 1.0233x; 1.0233x over previous
import math
import numpy as np
import ml_dtypes
from contextlib import ExitStack

import concourse.bass as bass
import concourse.mybir as mybir

F32 = mybir.dt.float32
F32R = mybir.dt.float32r
BF16 = mybir.dt.bfloat16
OP = mybir.AluOpType
AX = mybir.AxisListType.X
ACT = mybir.ActivationFunctionType

D = 2048
NH = 16
HD = 128
CH = 128
NDC = D // 128
HID = 5461
HIDP = 5504
NHC = HIDP // 128
HTAIL = HID - 42 * 128


def _build_slopes(n=NH):
    def pow2(n):
        start = 2.0 ** (-(2.0 ** (-(math.log2(n) - 3))))
        return [start * start ** i for i in range(n)]
    assert math.log2(n).is_integer()
    return np.asarray(pow2(n), dtype=np.float32)


def _decay_constants():
    slopes = _build_slopes()
    r32 = np.exp(-slopes.astype(np.float64)).astype(np.float32)
    r = r32.astype(np.float64)
    i = np.arange(CH, dtype=np.float64)
    q_decay = (r[:, None] ** (i[None, :] + 1.0)).astype(np.float32)
    k_decay = (r[:, None] ** (CH - 1.0 - i[None, :])).astype(np.float32)
    diff = i[:, None] - i[None, :]
    mask = diff >= 0
    dmask = np.where(mask[None, :, :],
                     r[:, None, None] ** np.where(mask, diff, 0.0)[None],
                     0.0).astype(np.float32)
    r_C = (r ** float(CH)).astype(np.float32)
    return dict(r=r32, q_decay=q_decay, k_decay=k_decay, dmask=dmask, r_C=r_C)


def _rotary_tables(t0, T):
    freq = ((1.0 / 1024.0) ** np.linspace(0.0, 1.0, HD // 4)).astype(np.float32)
    freq = np.concatenate([freq, np.zeros(HD // 4, dtype=np.float32)])
    t = np.arange(t0, t0 + T, dtype=np.float32)
    theta = (t[:, None] * freq[None, :]).astype(np.float32)
    cos = np.cos(theta.astype(np.float64)).astype(np.float32)
    sin = np.sin(theta.astype(np.float64)).astype(np.float32)
    return cos, sin

ENGS = ("sync", "gpsimd", "tensor", "vector", "scalar")


def _run_op(e, op):
    k = op[0]
    if k == "mm":
        return e.matmul(op[1], op[2], op[3], start=op[4], stop=op[5])
    if k == "tr":
        return e.transpose(op[1], op[2], op[3])
    if k == "copy":
        return e.tensor_copy(op[1], op[2])
    if k == "scopy":
        return e.copy(op[1], op[2])
    if k == "tt":
        return e.tensor_tensor(op[1], op[2], op[3], op[4])
    if k == "ts":
        return e.tensor_scalar(op[1], op[2], op[3], op[4], op[5], op[6])
    if k == "tss":
        return e.tensor_single_scalar(op[1], op[2], op[3], op[4])
    if k == "stt":
        return e.scalar_tensor_tensor(op[1], op[2], op[3], op[4], op[5], op[6])
    if k == "red":
        return e.tensor_reduce(op[1], op[2], AX, op[3], apply_absolute_value=op[4])
    if k == "act":
        return e.activation(op[1], op[2], op[3], bias=op[4], scale=op[5])
    if k == "recip":
        return e.reciprocal(op[1], op[2])
    if k == "memset":
        return e.memset(op[1], op[2])
    if k == "dma":
        return e.dma_start(out=op[1], in_=op[2])
    if k == "cc":
        return e.collective_compute("AllGather", OP.bypass,
                                    replica_groups=op[1], ins=[op[2]], outs=[op[3]])
    if k == "drain":
        return e.drain()
    if k == "nop":
        return e.nop()
    raise ValueError(k)


class Plan:
    def __init__(self, nc):
        self.nc = nc
        self.ops = []
        self.ctr = {}
        self.bank_w = {}
        self.bank_r = {}
        self.pending = {}

    def cur(self, s):
        return self.ctr.get(s.name, 0)

    def emit(self, eng, op, waits=(), incs=()):
        waits = [w for w in waits if w is not None]
        if eng in self.pending and op[0] != "nop":
            waits = self.pending.pop(eng) + waits
        incs = [i for i in incs if i is not None]
        for s, a in incs:
            self.ctr[s.name] = self.ctr.get(s.name, 0) + a
        self.ops.append((eng, op, waits, incs))

    def defer(self, eng, waits):
        self.pending.setdefault(eng, []).extend(
            [w for w in waits if w is not None])

    def flush(self, tag=None):
        ops, self.ops = self.ops, []
        if not ops:
            return
        import contextlib
        i_lo = self.nc.n_instructions()
        scope = self.nc.named_scope(tag) if tag else contextlib.nullcontext()
        with scope, self.nc.Block() as blk:
            for eng in ENGS:
                mine = [o for o in ops if o[0] == eng]
                if not mine:
                    continue

                def mk(mine=mine):
                    def section(e):
                        for _, op, waits, incs in mine:
                            for s, v in waits:
                                e.wait_ge(s, v)
                            inst = _run_op(e, op)
                            for s, a in incs:
                                inst.then_inc(s, a)
                    return section
                getattr(blk, eng)(mk())
        if not hasattr(self.nc, "_phase_ranges"):
            self.nc._phase_ranges = []
        self.nc._phase_ranges.append((tag, i_lo, self.nc.n_instructions()))

    def wbank(self, key):
        return self.bank_r.get(key)

    def set_w(self, key, sv):
        self.bank_w[key] = sv

    def rbank(self, key):
        return self.bank_w.get(key)

    def set_r(self, key, sv):
        self.bank_r[key] = sv


def build(NCH=8, NCORES=8, TQ=256, debug_outputs=()):
    TPC = NCH * CH
    NTQ = TPC // TQ
    SL = min(512, TPC)
    NSL = TPC // SL
    NTT = TQ // 128

    cdc = _decay_constants()
    r_C = [float(x) for x in cdc["r_C"]]

    nc = bass.Bass("TRN2", target_bir_lowering=False)

    ap = {}

    def ein(name, shape, dt=F32):
        ap[name] = nc.dram_tensor(name, shape, dt, kind="ExternalInput").ap()

    ein("x", [TPC, D])
    ein("wq", [D, D]); ein("wk", [D, D]); ein("wv", [D, D]); ein("wo", [D, D])
    ein("rmsw_row", [1, D])
    S1 = -(-NHC // NCORES)
    S2 = -(-NDC // NCORES)
    TN = 2 * S1 * NDC + S2 * NHC
    ein("w1s", [S1 * 128, D]); ein("w3s", [S1 * 128, D])
    ein("w2s", [S2 * 128, HID])
    ein("identf", [128, 128])
    ein("identb", [128, 128], BF16)
    ein("swap64", [128, 128])
    ein("rotA", [128, TPC])
    ein("rotB", [128, TPC])
    ein("dmaskT", [128, NH * CH])
    ein("qdec", [128, NH * CH])
    ein("kdec", [128, NH * CH])
    ein("coef", [128, NCORES * NH])
    ein("onescol", [128, 1])
    ein("onesrow", [1, 128])

    out_t = nc.dram_tensor("out", [TPC, D], F32, kind="ExternalOutput").ap()
    dbg = {}
    for nm in debug_outputs:
        shp = {"q": [NCH, 128, D], "k": [NCH, 128, D], "v": [NCH, 128, D],
               "qrot": [NCH, 128, D], "krot": [NCH, 128, D],
               "attn": [NCH, 128, D], "attn_n": [NCH, 128, D],
               "y1": [NDC, 128, TPC], "x2q": [NTQ, NDC, 128, TQ],
               "gq": [NTQ, NHC, 128, TQ], "sall": [NCORES * 128, NH * HD],
               "kvin": [128, NH * HD], "xfm": [NDC, 128, TPC],
               "svals": [1, 8], "t1q": [NHC, NDC, 128, 128],
               "ln1": [1, 2 * 1024]}[nm]
        dbg[nm] = nc.dram_tensor("dbg_" + nm, shp, F32, kind="ExternalOutput").ap()

    x_fm_d = nc.dram_tensor("x_fm_d", [NDC, 128, TPC], F32, kind="Internal").ap()
    q_d = nc.dram_tensor("q_d", [NCH, 128, D], F32, kind="Internal").ap()
    k_d = nc.dram_tensor("k_d", [NCH, 128, D], F32, kind="Internal").ap()
    v_d = nc.dram_tensor("v_d", [NCH, 128, D], F32, kind="Internal").ap()
    qt_d = nc.dram_tensor("qt_d", [NCH, 128, D], F32, kind="Internal").ap()
    afm_d = nc.dram_tensor("afm_d", [NDC, 128, TPC], F32, kind="Internal").ap()
    y1_d = nc.dram_tensor("y1_d", [NDC, 128, TPC], F32, kind="Internal").ap()
    AS = "Shared" if NCORES > 4 else "Local"
    ln1_d = nc.dram_tensor("ln1_d", [1, 2 * 1024], F32, kind="Internal").ap()
    cc_in = nc.dram_tensor("cc_in", [128, NH * HD], F32, kind="Internal").ap()
    cc_out = nc.dram_tensor("cc_out", [NCORES * 128, NH * HD], F32, kind="Internal", addr_space=AS).ap()
    cc2_in = nc.dram_tensor("cc2_in", [1, 4], F32, kind="Internal").ap()
    cc2_out = nc.dram_tensor("cc2_out", [NCORES, 4], F32, kind="Internal", addr_space=AS).ap()
    cct_in = nc.dram_tensor("cct_in", [TN, 128, 128], BF16, kind="Internal").ap()
    cct_out = nc.dram_tensor("cct_out", [NCORES, TN, 128, 128], BF16,
                             kind="Internal", addr_space=AS).ap()

    with ExitStack() as ctx:
        def sbuf(name, shape, dt=F32):
            return ctx.enter_context(nc.sbuf_tensor(name, shape, dt))

        def sem(name):
            return ctx.enter_context(nc.semaphore(name))

        c_identf = sbuf("c_identf", [128, 128])
        c_identb = sbuf("c_identb", [128, 128], BF16)
        c_swap = sbuf("c_swap", [128, 128])
        c_onesc = sbuf("c_onesc", [128, 1])
        c_onesr = sbuf("c_onesr", [1, 128])
        stage = sbuf("stage", [128, D])
        colbuf = sbuf("colbuf", [128, 16])
        svals = sbuf("svals", [1, 16])
        scol = sbuf("scol", [128, 12])
        bca = sbuf("bca", [128, TQ])
        bcb = sbuf("bcb", [128, TQ])
        gtmp = sbuf("gtmp", [128, TQ])
        gtmp2 = sbuf("gtmp2", [128, TQ])

        pbs = [ctx.enter_context(nc.psum_tensor(f"pb{i}", [128, 512], F32))
               for i in range(8)]

        s_cin = sem("s_cin")
        s_w = sem("s_w")
        s_w2 = sem("s_w2")
        s_w3 = sem("s_w3")
        s_w4 = sem("s_w4")
        s_w5 = sem("s_w5")
        s_w6 = sem("s_w6")
        s_ld4 = sem("s_ld4")
        s_ld1 = sem("s_ld1")
        s_ld2 = sem("s_ld2")
        s_ld3 = sem("s_ld3")
        s_ldA = sem("s_ldA")
        s_spA = sem("s_spA")
        s_spA2 = sem("s_spA2")
        s_spB = sem("s_spB")
        s_spC = sem("s_spC")
        s_spD = sem("s_spD")
        s_spE = sem("s_spE")
        s_spF = sem("s_spF")
        s_dbg = sem("s_dbg")
        s_outd = sem("s_outd")
        s_cast = sem("s_cast")
        s_cast2 = sem("s_cast2")
        s_mm = sem("s_mm")
        s_ve = sem("s_ve")
        s_sc = sem("s_sc")
        s_gp = sem("s_gp")
        s_cc = sem("s_cc")

        pl = Plan(nc)

        def MM(bank_ap, key, lhsT, rhs, start, stop, waits=(), mark=False):
            w = list(waits)
            if start:
                w.append(pl.wbank(key))
            incs = [(s_mm, 1)] if (stop or mark) else []
            pl.emit("tensor", ("mm", bank_ap, lhsT, rhs, start, stop), w, incs)
            if stop:
                pl.set_w(key, (s_mm, pl.cur(s_mm)))

        def TR(bank_ap, key, in_ap, ident, waits=()):
            w = list(waits)
            w.append(pl.wbank(key))
            pl.emit("tensor", ("tr", bank_ap, in_ap, ident), w, [(s_mm, 1)])
            pl.set_w(key, (s_mm, pl.cur(s_mm)))

        def EVAC(eng, op, key, waits=()):
            w = list(waits)
            w.append(pl.rbank(key))
            isem = s_ve if eng == "vector" else s_sc
            pl.emit(eng, op, w, [(isem, 1)])
            pl.set_r(key, (isem, pl.cur(isem)))
            return (isem, pl.cur(isem))

        def V(op, waits=(), inc=True):
            pl.emit("vector", op, waits, [(s_ve, 1)] if inc else [])
            return (s_ve, pl.cur(s_ve))

        def VD():
            pl.emit("vector", ("drain",), [], [])

        def SC(op, waits=(), inc=True):
            pl.emit("scalar", op, waits, [(s_sc, 1)] if inc else [])
            return (s_sc, pl.cur(s_sc))

        def SCD():
            pl.emit("scalar", ("drain",), [], [])

        def G(op, waits=(), inc=True):
            pl.emit("gpsimd", op, waits, [(s_gp, 1)] if inc else [])
            return (s_gp, pl.cur(s_gp))

        def GPD():
            pl.emit("gpsimd", ("drain",), [], [])

        def GD(out_ap, in_ap, semx, waits=()):
            pl.emit("gpsimd", ("dma", out_ap, in_ap), waits, [(semx, 16)])
            return (semx, pl.cur(semx))

        def GC(out_ap, in_ap, waits=()):
            pl.emit("gpsimd", ("dma", out_ap, in_ap), waits, [(s_cast, 16)])
            return (s_cast, pl.cur(s_cast))

        def SY(out_ap, in_ap, waits=(), semx=None):
            semx = semx or s_cin
            pl.emit("sync", ("dma", out_ap, in_ap), waits, [(semx, 16)])
            return (semx, pl.cur(semx))

        def SCY(out_ap, in_ap, waits=(), semx=None):
            semx = semx or s_w5
            pl.emit("scalar", ("dma", out_ap, in_ap), waits, [(semx, 16)])
            return (semx, pl.cur(semx))

        def FENCE(*waits):
            pl.emit("vector", ("nop",), list(waits), [])

        def FENCE_ALL():
            allsems = [s_cin, s_w, s_w2, s_w3, s_w4, s_w5, s_w6, s_ld4, s_ld1, s_ld2, s_ld3, s_ldA, s_spA,
                       s_spA2, s_spB, s_spC, s_spD, s_spE, s_spF, s_dbg, s_outd,
                       s_cast, s_cast2]
            pl.emit("vector", ("nop",),
                    [(x, pl.cur(x)) for x in allsems if pl.cur(x) > 0], [])
            pl.flush("B0")

        def bc_heads(const_ap, reps):
            return const_ap.unsqueeze(1).broadcast_to([128, reps, 128])

        def bcast_row(row_ap, width, bank_key, dst_ap, waits=()):
            done = None
            for off in range(0, width, 512):
                wd = min(512, width - off)
                MM(pbs[bank_key][:, 0:wd], bank_key, c_onesr[:],
                   row_ap[:, off:off + wd], True, True, waits=waits if off == 0 else ())
                done = EVAC("vector",
                            ("copy", dst_ap[:, off:off + wd], pbs[bank_key][:, 0:wd]),
                            bank_key)
            return done

        for nm, dst in [("identf", c_identf), ("identb", c_identb),
                        ("swap64", c_swap), ("onescol", c_onesc),
                        ("onesrow", c_onesr)]:
            SY(dst[:], ap[nm])
        FENCE((s_cin, pl.cur(s_cin)))
        pl.flush("B0")

        if True:

            b8l_ctx = ExitStack()
            wn8 = [b8l_ctx.enter_context(nc.sbuf_tensor(f"wn8{i}", [128, D], F32))
                   for i in range(2)]
            wq8 = [b8l_ctx.enter_context(nc.sbuf_tensor(f"wq8{i}", [128, D], BF16))
                   for i in range(2)]
            wst8 = [b8l_ctx.enter_context(nc.sbuf_tensor(f"wst8{i}", [128, D], BF16))
                    for i in range(2)]
            ws8 = b8l_ctx.enter_context(nc.sbuf_tensor("ws8", [128, D], F32))
            svp_sb = b8l_ctx.enter_context(nc.sbuf_tensor("svp_sb", [128, 4], F32))
            xhl_ctx = ExitStack()
            xhi2 = xhl_ctx.enter_context(nc.sbuf_tensor("xhi2", [128, NDC * TPC], BF16))
            xlo2 = xhl_ctx.enter_context(nc.sbuf_tensor("xlo2", [128, NDC * TPC], BF16))
            xfm_ctx = ExitStack()
            xfm = xfm_ctx.enter_context(nc.sbuf_tensor("xfm", [128, NDC * TPC], F32))
            stage2 = xfm_ctx.enter_context(nc.sbuf_tensor("xstg2", [128, D], F32))
            if True:
                def xfm_t(dc):
                    return xfm[:][:, dc * TPC:(dc + 1) * TPC]

                for tb in range(NCH):
                    lx = SY(stage2[:], ap["x"][tb * 128:(tb + 1) * 128, :],
                            semx=s_w)
                    for dc in range(NDC):
                        key = (tb * NDC + dc) % 4
                        TR(pbs[key][:, 0:128], key,
                           stage2[:][:, dc * 128:(dc + 1) * 128],
                           c_identf[:], waits=[lx] if dc == 0 else ())
                        EVAC("vector",
                             ("copy", xfm_t(dc)[:, tb * 128:(tb + 1) * 128],
                              pbs[key][:, 0:128]), key)
                    pl.defer("sync", [(s_mm, pl.cur(s_mm))])
                xfmv = (s_ve, pl.cur(s_ve))
                for dc in range(NDC):
                    GD(x_fm_d[dc], xfm_t(dc), s_spA, waits=[xfmv] if dc == 0 else ())
                if "xfm" in dbg:
                    for dc in range(NDC):
                        GD(dbg["xfm"][dc], xfm_t(dc), s_dbg)
                pl.flush("B1")

                for dc in range(NDC):
                    V(("copy", xhi2[:][:, dc * TPC:(dc + 1) * TPC], xfm_t(dc)),
                      waits=[xfmv] if dc == 0 else ())
                    VD()
                    V(("tt", xlo2[:][:, dc * TPC:(dc + 1) * TPC], xfm_t(dc),
                       xhi2[:][:, dc * TPC:(dc + 1) * TPC], OP.subtract))
                    VD()
                xsplv = (s_ve, pl.cur(s_ve))
                pl.flush("B1s")
            FENCE_ALL()
            xfm_ctx.close()

            mats = [("w1s", HID, D, 0), ("w3s", HID, D, S1 * NDC),
                    ("w2s", D, HID, 2 * S1 * NDC)]
            b8_steps = []
            b8_when = []
            b8_state = dict(red={0: None, 1: None},
                            q={0: None, 1: None}, tr={0: None, 1: None},
                            st={0: None, 1: None}, scolv=None, pc=0)

            def _mk_passA(mi, wt, sl_, c0, c1):
                def step():
                    st_ = b8_state
                    par = st_["pc"] % 2
                    st_["pc"] += 1
                    lw = SCY(wn8[par][:, 0:c1 - c0],
                             ap[wt][sl_ * 128:(sl_ + 1) * 128, c0:c1],
                             semx=s_w5 if par == 0 else s_w6,
                             waits=[st_["red"][par]])
                    V(("red", colbuf[:][:, 0:1], wn8[par][:][:, 0:c1 - c0],
                       OP.add, True), waits=[lw])
                    VD()
                    V(("tt", colbuf[:][:, 4 + mi:5 + mi],
                       colbuf[:][:, 4 + mi:5 + mi], colbuf[:][:, 0:1], OP.add))
                    VD()
                    st_["red"][par] = (s_ve, pl.cur(s_ve))
                return step

            def _mk_partial_cc():
                def step():
                    for mi, (wt, nrow, ncol, _) in enumerate(mats):
                        MM(pbs[7][0:1, 0:1], 7, c_onesc[:],
                           colbuf[:][:, 4 + mi:5 + mi], True, True,
                           waits=[(s_ve, pl.cur(s_ve))] if mi == 0 else ())
                        EVAC("vector", ("copy", svals[:][:, mi:mi + 1],
                                        pbs[7][0:1, 0:1]), 7)
                        VD()
                    gdp = GD(cc2_in, svals[:][0:1, 0:4], s_spB,
                             waits=[(s_ve, pl.cur(s_ve))])
                    pl.emit("gpsimd",
                            ("cc", [list(range(NCORES))], cc2_in, cc2_out),
                            [gdp], [(s_cc, 1)])
                return step

            def _mk_scalc_final():
                def step():
                    pl.emit("scalar", ("dma", svp_sb[0:NCORES, 0:4], cc2_out),
                            [(s_cc, pl.cur(s_cc))], [(s_ld4, 16)])
                    MM(pbs[7][0:1, 0:4], 7, c_onesc[0:NCORES, 0:1],
                       svp_sb[0:NCORES, 0:4], True, True,
                       waits=[(s_ld4, pl.cur(s_ld4))])
                    EVAC("vector", ("copy", svals[:][:, 0:4],
                                    pbs[7][0:1, 0:4]), 7)
                    VD()
                    for mi, (wt, nrow, ncol, _) in enumerate(mats):
                        V(("ts", svals[:][:, 4 + mi:5 + mi],
                           svals[:][:, mi:mi + 1],
                           1.0 / (nrow * ncol), None, OP.mult, OP.bypass))
                        VD()
                        V(("ts", svals[:][:, 8 + mi:9 + mi],
                           svals[:][:, 4 + mi:5 + mi],
                           0.5, None, OP.mult, OP.bypass))
                        VD()
                    for mi in range(3):
                        bcast_row(svals[:][:, 4 + mi:5 + mi], 1, 7,
                                  scol[:][:, mi:mi + 1],
                                  waits=[(s_ve, pl.cur(s_ve))] if mi == 0
                                  else ())
                        bcast_row(svals[:][:, 8 + mi:9 + mi], 1, 7,
                                  scol[:][:, 3 + mi:4 + mi])
                        V(("ts", scol[:][:, 6 + mi:7 + mi],
                           scol[:][:, 3 + mi:4 + mi],
                           -1.0, None, OP.mult, OP.bypass),
                          waits=[(s_ve, pl.cur(s_ve))])
                        VD()
                    b8_state["scolv"] = (s_ve, pl.cur(s_ve))
                return step

            def _mk_passB(mi, wt, sl_, c0, c1, t0, base):
                def step():
                    st_ = b8_state
                    par = st_["pc"] % 2
                    st_["pc"] += 1
                    ntp = (c1 - c0 + 127) // 128
                    ncp = ntp * 128
                    prev = st_["q"][par] or st_["red"][par]
                    mzv = None
                    if c1 - c0 < ncp:
                        mzv = V(("memset", wn8[par][:, c1 - c0:ncp], 0.0),
                                waits=[prev])
                        VD()
                    lw = SCY(wn8[par][:, 0:c1 - c0],
                             ap[wt][sl_ * 128:(sl_ + 1) * 128, c0:c1],
                             semx=s_w5 if par == 0 else s_w6,
                             waits=[mzv if mzv else prev])
                    V(("tss", ws8[:][:, 0:ncp], wn8[par][:][:, 0:ncp],
                       scol[:][:, 6 + mi:7 + mi], OP.is_lt),
                      waits=[lw, st_["tr"][par], st_["scolv"]])
                    VD()
                    qv = V(("stt", wq8[par][:][:, 0:ncp], wn8[par][:][:, 0:ncp],
                            scol[:][:, 3 + mi:4 + mi], ws8[:][:, 0:ncp],
                            OP.is_gt, OP.subtract))
                    st_["q"][par] = qv
                    for ti in range(ntp):
                        bk = 6 + ti % 2
                        TR(pbs[bk][:, 0:512].bitcast(BF16)[:, 0:128], bk,
                           wq8[par][:][:, ti * 128:(ti + 1) * 128], c_identb[:],
                           waits=[w for w in [qv if ti == 0 else None,
                                              st_["st"][par] if ti == 0 else None]
                                  if w])
                        eng = "scalar" if ti % 2 == 0 else "vector"
                        EVAC(eng, (("scopy" if ti % 2 == 0 else "copy"),
                                   wst8[par][:][:, ti * 128:(ti + 1) * 128],
                                   pbs[bk][:, 0:512].bitcast(BF16)[:, 0:128]), bk)
                    st_["tr"][par] = (s_mm, pl.cur(s_mm))
                    stv = [(s_sc, pl.cur(s_sc)), (s_ve, pl.cur(s_ve))]
                    gv = GD(cct_in[base + t0:base + t0 + ntp]
                            .rearrange("a p t -> p a t"),
                            wst8[par][:][:, 0:ncp]
                            .rearrange("p (a t) -> p a t", a=ntp),
                            s_spD if par == 0 else s_spE, waits=stv)
                    st_["st"][par] = gv
                return step

            V(("memset", svals[:], 0.0))
            for mi in range(3):
                V(("memset", colbuf[:][:, 4 + mi:5 + mi], 0.0))
            pl.flush("B8z")
            stepsA = []
            for mi, (wt, nrow, ncol, _) in enumerate(mats):
                nsl = S1 if mi < 2 else S2
                for sl_ in range(nsl):
                    for c0 in range(0, ncol, D):
                        stepsA.append(_mk_passA(mi, wt, sl_, c0,
                                                min(c0 + D, ncol)))
            stepsA.append(_mk_partial_cc())
            stepsB = []
            for mi, (wt, nrow, ncol, base) in enumerate(mats):
                nsl = S1 if mi < 2 else S2
                npc = NHC if mi == 2 else NDC
                for sl_ in range(nsl):
                    for pidx, c0 in enumerate(range(0, ncol, D)):
                        stepsB.append(_mk_passB(mi, wt, sl_, c0,
                                                min(c0 + D, ncol),
                                                pidx * (D // 128),
                                                base + sl_ * npc))
            for i in range(len(stepsA)):
                b8_steps.append(stepsA[i])
                b8_when.append(i // (-(-len(stepsA) // 4)))
            b8_steps.append(_mk_scalc_final())
            b8_when.append(14)
            for i in range(len(stepsB)):
                b8_steps.append(stepsB[i])
                b8_when.append(15 + i // (-(-len(stepsB) // 8)))
            b8_i = [0]

            def b8_pump(g8):
                while (b8_i[0] < len(b8_steps)
                       and (g8 is None or b8_when[b8_i[0]] <= g8)):
                    b8_steps[b8_i[0]]()
                    b8_i[0] += 1

            w8_ctx = ExitStack()
            w8f = [w8_ctx.enter_context(nc.sbuf_tensor(f"w8f{i}", [128, NDC * 256], F32))
                   for i in range(2)]
            w8h = [w8_ctx.enter_context(nc.sbuf_tensor(f"w8h{i}", [128, NDC * 256], BF16))
                   for i in range(2)]
            w8l = [w8_ctx.enter_context(nc.sbuf_tensor(f"w8l{i}", [128, NDC * 256], BF16))
                   for i in range(2)]
            stg2b = w8_ctx.enter_context(nc.sbuf_tensor("stg2b", [128, D], F32))
            if True:
                spills = {0: None, 1: None}
                jobs = []
                for wt, dst_d in [("wq", q_d), ("wk", k_d), ("wv", v_d)]:
                    for e8 in range(8):
                        jobs.append((wt, dst_d, e8))
                NJ = len(jobs)
                lwh = {}
                spldone = {}
                mmcons = {}

                def b2_load(j):
                    wt, _, e8 = jobs[j]
                    par = j % 2
                    lwh[j] = SY(w8f[par][:].rearrange("p (c e) -> p c e", c=NDC),
                                ap[wt][:, e8 * 256:(e8 + 1) * 256]
                                .rearrange("(c p) e -> p c e", p=128),
                                semx=s_w if par == 0 else s_w2,
                                waits=[spldone.get(j - 2)])

                def b2_split(j):
                    par = j % 2
                    V(("copy", w8h[par][:], w8f[par][:]),
                      waits=[lwh[j], mmcons.get(j - 2)])
                    VD()
                    V(("tt", w8l[par][:], w8f[par][:], w8h[par][:],
                       OP.subtract))
                    VD()
                    spldone[j] = (s_ve, pl.cur(s_ve))

                b2_load(0)
                b2_load(1)
                b2_split(0)
                for j in range(NJ):
                    wt, dst_d, e8 = jobs[j]
                    par = j % 2
                    if j + 2 < NJ:
                        b2_load(j + 2)
                    if j + 1 < NJ:
                        b2_split(j + 1)
                    for ebl2 in range(2):
                        eb = e8 * 2 + ebl2
                        ebpar = eb % 2
                        stg = stage if ebpar == 0 else stg2b
                        for sl in range(NSL):
                            bk = ebpar * 2 + sl
                            for dc in range(NDC):
                                wtile_h = w8h[par][:][:, dc * 256 + ebl2 * 128:
                                                      dc * 256 + (ebl2 + 1) * 128]
                                wtile_l = w8l[par][:][:, dc * 256 + ebl2 * 128:
                                                      dc * 256 + (ebl2 + 1) * 128]
                                xh = xhi2[:][:, dc * TPC + sl * SL:
                                             dc * TPC + (sl + 1) * SL]
                                xl = xlo2[:][:, dc * TPC + sl * SL:
                                             dc * TPC + (sl + 1) * SL]
                                MM(pbs[bk][:, 0:SL], bk, wtile_h, xh,
                                   dc == 0, False,
                                   waits=[spldone[j], xsplv] if dc == 0 else ())
                                MM(pbs[bk][:, 0:SL], bk, wtile_h, xl,
                                   False, False)
                                MM(pbs[bk][:, 0:SL], bk, wtile_l, xh,
                                   False, dc == NDC - 1)
                            EVAC("vector",
                                 ("copy", stg[:][:, sl * SL:(sl + 1) * SL],
                                  pbs[bk][:, 0:SL]), bk,
                                 waits=[spills[ebpar]]
                                 if (sl == 0 and spills[ebpar]) else ())
                        ev = (s_ve, pl.cur(s_ve))
                        gv = GD(dst_d[:, :, eb * 128:(eb + 1) * 128]
                                .rearrange("c p t -> p c t"),
                                stg[:][:, 0:TPC]
                                .rearrange("p (c t) -> p c t", c=NCH),
                                s_spA if ebpar == 0 else s_spA2,
                                waits=[ev])
                        spills[ebpar] = gv
                    mmcons[j] = (s_mm, pl.cur(s_mm))
                    b8_pump(j)
                    if j % 4 == 3:
                        pl.flush(f"B2p{j // 4}")
                if any(n in dbg for n in ("q", "k", "v")):
                    for nm, src in [("q", q_d), ("k", k_d), ("v", v_d)]:
                        if nm in dbg:
                            GD(dbg[nm].rearrange("a b c -> (a b) c"),
                               src.rearrange("a b c -> (a b) c"), s_dbg)
                    pl.flush("B2d")
            FENCE_ALL()
            w8_ctx.close()
            while b8_i[0] < len(b8_steps):
                nxt = b8_i[0] + 6
                while b8_i[0] < min(nxt, len(b8_steps)):
                    b8_steps[b8_i[0]]()
                    b8_i[0] += 1
                pl.flush("B8r")
            FENCE_ALL()
            xhl_ctx.close()
            b8l_ctx.close()

            attn_ctx = ExitStack()
            c_dmaskT = attn_ctx.enter_context(nc.sbuf_tensor("c_dmaskT", [128, NH * CH], F32))
            c_qdec = attn_ctx.enter_context(nc.sbuf_tensor("c_qdec", [128, NH * CH], F32))
            c_kdec = attn_ctx.enter_context(nc.sbuf_tensor("c_kdec", [128, NH * CH], F32))
            c_rotA = attn_ctx.enter_context(nc.sbuf_tensor("c_rotA", [128, TPC], F32))
            c_rotB = attn_ctx.enter_context(nc.sbuf_tensor("c_rotB", [128, TPC], F32))
            c_coef = attn_ctx.enter_context(nc.sbuf_tensor("c_coef", [128, NCORES * NH], F32))
            c_rmsw = attn_ctx.enter_context(nc.sbuf_tensor("c_rmsw", [1, D], F32))
            pkv = attn_ctx.enter_context(nc.sbuf_tensor("pkv", [128, NH * HD], F32))
            for nm, dst in [("dmaskT", c_dmaskT), ("qdec", c_qdec),
                            ("kdec", c_kdec), ("rotA", c_rotA), ("rotB", c_rotB),
                            ("coef", c_coef), ("rmsw_row", c_rmsw)]:
                SY(dst[:], ap[nm])
            cin0 = (s_cin, pl.cur(s_cin))
            FENCE(cin0)
            pl.flush("B2")
            atm_ctx = ExitStack()
            atm = atm_ctx.enter_context(nc.sbuf_tensor("atm", [128, NCH * D], F32))
            if True:
                def atm_t(c):
                    return atm[:][:, c * D:(c + 1) * D]

                chunk_ctx = ExitStack()
                qch = chunk_ctx.enter_context(nc.sbuf_tensor("qch", [128, D], F32))
                kch = chunk_ctx.enter_context(nc.sbuf_tensor("kch", [128, D], F32))
                vch = chunk_ctx.enter_context(nc.sbuf_tensor("vch", [128, D], F32))
                qtch = chunk_ctx.enter_context(nc.sbuf_tensor("qtch", [128, D], F32))
                ktch = chunk_ctx.enter_context(nc.sbuf_tensor("ktch", [128, D], F32))
                kttm = chunk_ctx.enter_context(nc.sbuf_tensor("kttm", [128, D], F32))
                vtm = chunk_ctx.enter_context(nc.sbuf_tensor("vtm", [128, D], F32))
                stage2 = chunk_ctx.enter_context(nc.sbuf_tensor("rstg2", [128, D], F32))
                qch2 = chunk_ctx.enter_context(nc.sbuf_tensor("qch2", [128, D], F32))
                kch2 = chunk_ctx.enter_context(nc.sbuf_tensor("kch2", [128, D], F32))
                if True:
                    FENCE((s_spA, pl.cur(s_spA)), (s_spA2, pl.cur(s_spA2)))
                    V(("memset", pkv[:], 0.0))
                    pzero = (s_ve, pl.cur(s_ve))
                    qchP = [qch, qch2]
                    kchP = [kch, kch2]
                    lds = {}
                    chkcons = {}
                    vcons = {}
                    lds[0] = (GD(qch[:], q_d[0], s_ld1),
                              GD(kch[:], k_d[0], s_ld2),
                              GD(vch[:], v_d[0], s_ld3))
                    if NCH > 1:
                        lds[1] = (GD(qch2[:], q_d[1], s_ldA),
                                  GD(kch2[:], k_d[1], s_spA), None)
                    for c in range(NCH):
                        par3 = c % 2
                        qb = qchP[par3]
                        kb = kchP[par3]
                        for buf, ldw in ((qb, lds[c][0]), (kb, lds[c][1])):
                            for gi in range(4):
                                MM(pbs[5][:, 0:512], 5, c_swap[:],
                                   buf[:][:, gi * 512:(gi + 1) * 512], True, True,
                                   waits=[ldw] if gi == 0 else ())
                                EVAC("vector",
                                     ("tt", stage2[:][:, gi * 512:(gi + 1) * 512]
                                      .rearrange("p (a t) -> p a t", a=4),
                                      pbs[5][:].rearrange("p (a t) -> p a t", a=4),
                                      bc_heads(c_rotB[:][:, c * 128:(c + 1) * 128], 4),
                                      OP.mult), 5)
                            swm = (s_mm, pl.cur(s_mm))
                            V(("tt", buf[:].rearrange("p (a t) -> p a t", a=NH),
                               buf[:].rearrange("p (a t) -> p a t", a=NH),
                               bc_heads(c_rotA[:][:, c * 128:(c + 1) * 128], NH),
                               OP.mult), waits=[swm])
                            VD()
                            V(("tt", buf[:], buf[:], stage2[:], OP.add))
                            VD()
                        rotv = (s_ve, pl.cur(s_ve))
                        if "qrot" in dbg:
                            GD(dbg["qrot"][c], qb[:], s_dbg, waits=[rotv])
                        if "krot" in dbg:
                            GD(dbg["krot"][c], kb[:], s_dbg, waits=[rotv])
                        V(("tt", qtch[:], qb[:], c_qdec[:], OP.mult),
                          waits=[rotv, (s_spB, pl.cur(s_spB)),
                                 chkcons.get(c - 1)])
                        V(("tt", ktch[:], kb[:], c_kdec[:], OP.mult))
                        tilv = (s_ve, pl.cur(s_ve))
                        GD(qt_d[c], qtch[:], s_spB, waits=[tilv])
                        for h in range(NH):
                            hsl = slice(h * 128, (h + 1) * 128)
                            bk = 5 if h % 2 == 0 else 3
                            bk2 = 7 if h % 2 == 0 else 6
                            TR(pbs[bk][:, 0:128], bk, ktch[:][:, hsl], c_identf[:],
                               waits=[tilv] if h == 0 else ())
                            EVAC("scalar", ("scopy", kttm[:][:, hsl],
                                            pbs[bk][:, 0:128]), bk,
                                 waits=[chkcons.get(c - 1)] if h == 0 else ())
                            TR(pbs[bk2][:, 0:128], bk2, vch[:][:, hsl], c_identf[:],
                               waits=[lds[c][2]] if h == 0 else ())
                            EVAC("vector", ("copy", vtm[:][:, hsl],
                                            pbs[bk2][:, 0:128]), bk2,
                                 waits=[chkcons.get(c - 1)] if h == 0 else ())
                        trv = (s_sc, pl.cur(s_sc))
                        trv2 = (s_ve, pl.cur(s_ve))
                        vcons[c] = (s_mm, pl.cur(s_mm))
                        if c + 1 < NCH:
                            ldv = GD(vch[:], v_d[c + 1], s_ld3,
                                     waits=[vcons[c]])
                            lds[c + 1] = (lds[c + 1][0], lds[c + 1][1], ldv) \
                                if c + 1 in lds else (None, None, ldv)
                        for h in range(NH):
                            hsl = slice(h * 128, (h + 1) * 128)
                            bs = 0 if h % 2 == 0 else 6
                            ba = 1 if h % 2 == 0 else 4
                            bd = 2 if h % 2 == 0 else 7
                            scmv = stage[:][:, (h % 2) * 128:(h % 2) * 128 + 128]
                            MM(pbs[bs][:, 0:128], bs, kb[:][:, hsl], qb[:][:, hsl],
                               True, True)
                            scv = EVAC("vector", ("tt", scmv, pbs[bs][:, 0:128],
                                                  c_dmaskT[:][:, hsl], OP.mult), bs,
                                       waits=[chkcons.get(c - 1)]
                                       if h == 0 else ())
                            MM(pbs[ba][:, 0:128], ba, scmv, vtm[:][:, hsl],
                               True, False,
                               waits=[scv] + ([trv, trv2, pzero] if h == 0 else []))
                            MM(pbs[ba][:, 0:128], ba, qtch[:][:, hsl], pkv[:][:, hsl],
                               False, True)
                            MM(pbs[bd][:, 0:128], bd, kttm[:][:, hsl], vtm[:][:, hsl],
                               True, True)
                            EVAC("scalar", ("scopy", atm_t(c)[:, hsl],
                                            pbs[ba][:, 0:128]), ba)
                            EVAC("vector", ("stt", pkv[:][:, hsl], pkv[:][:, hsl],
                                            r_C[h], pbs[bd][:, 0:128], OP.mult, OP.add), bd)
                        chkcons[c] = (s_mm, pl.cur(s_mm))
                        if c + 2 < NCH:
                            lq3 = GD(qchP[par3][:], q_d[c + 2],
                                     s_ld1 if par3 == 0 else s_ldA,
                                     waits=[chkcons[c]])
                            lk3 = GD(kchP[par3][:], k_d[c + 2],
                                     s_ld2 if par3 == 0 else s_spA,
                                     waits=[chkcons[c]])
                            old = lds.get(c + 2)
                            lds[c + 2] = (lq3, lk3, old[2] if old else None)
                        if c % 2 == 1:
                            pl.flush("B3")
                    pl.flush("B3")

                    gs = GD(cc_in, pkv[:], s_spB)
                    pl.emit("gpsimd", ("cc", [list(range(NCORES))], cc_in, cc_out),
                            [gs], [(s_cc, 1)])
                    pl.flush("B4")
                    if "sall" in dbg:
                        GD(dbg["sall"], cc_out, s_dbg, waits=[(s_cc, pl.cur(s_cc))])
                    for j in range(NCORES):
                        lj = GD(stage[:], cc_out[j * 128:(j + 1) * 128, :],
                                s_ld1, waits=[(s_cc, pl.cur(s_cc))] if j == 0 else ())
                        for h in range(NH):
                            hsl = slice(h * 128, (h + 1) * 128)
                            if j == 0:
                                V(("ts", pkv[:][:, hsl], stage[:][:, hsl],
                                   c_coef[:][:, 0 * NH + h: 0 * NH + h + 1], None,
                                   OP.mult, OP.bypass),
                                  waits=[lj] if h == 0 else (), inc=(h == NH - 1))
                            else:
                                V(("stt", pkv[:][:, hsl], stage[:][:, hsl],
                                   c_coef[:][:, j * NH + h: j * NH + h + 1],
                                   pkv[:][:, hsl], OP.mult, OP.add),
                                  waits=[lj] if h == 0 else (), inc=(h == NH - 1))
                        pl.defer("gpsimd", [(s_ve, pl.cur(s_ve))])
                    pl.emit("gpsimd",
                            ("cc", [list(range(NCORES))], cct_in, cct_out),
                            [(s_spD, pl.cur(s_spD)), (s_spE, pl.cur(s_spE))],
                            [(s_cc, 1)])
                    pl.flush("B4")
                    if "kvin" in dbg:
                        GD(dbg["kvin"], pkv[:], s_dbg)
                        FENCE((s_dbg, pl.cur(s_dbg)))
                        pl.flush("B4")

                    FENCE((s_spB, pl.cur(s_spB)))
                    pl.flush("B5")
                    qb5 = [qtch, qch]
                    lq5 = {}
                    cons5 = {}
                    mv = (s_ve, pl.cur(s_ve))
                    lq5[0] = GD(qb5[0][:], qt_d[0], s_ld1)
                    if NCH > 1:
                        lq5[1] = GD(qb5[1][:], qt_d[1], s_ld2)
                    for c in range(NCH):
                        for h in range(NH):
                            hsl = slice(h * 128, (h + 1) * 128)
                            bk = h % 4
                            MM(pbs[bk][:, 0:128], bk, qb5[c % 2][:][:, hsl],
                               pkv[:][:, hsl],
                               True, True, waits=[lq5[c], mv] if h == 0 else ())
                            EVAC("vector", ("tt", atm_t(c)[:, hsl], atm_t(c)[:, hsl],
                                            pbs[bk][:, 0:128], OP.add), bk)
                        cons5[c] = (s_mm, pl.cur(s_mm))
                        if c + 2 < NCH:
                            lq5[c + 2] = GD(qb5[c % 2][:], qt_d[c + 2],
                                            s_ld1 if c % 2 == 0 else s_ld2,
                                            waits=[cons5[c]])
                        if c < NCH - 1:
                            for h in range(NH):
                                hsl = slice(h * 128, (h + 1) * 128)
                                V(("ts", pkv[:][:, hsl], pkv[:][:, hsl], r_C[h],
                                   None, OP.mult, OP.bypass),
                                  waits=[cons5[c]] if h == 0 else (),
                                  inc=(h == NH - 1))
                            mv = (s_ve, pl.cur(s_ve))
                        if c % 4 == 3:
                            pl.flush("B5")
                    if "attn" in dbg:
                        for c in range(NCH):
                            GD(dbg["attn"][c], atm_t(c), s_dbg)
                        pl.flush("B5")

                FENCE_ALL()
                chunk_ctx.close()
                b6_ctx = ExitStack()
                rmsw_b = b6_ctx.enter_context(nc.sbuf_tensor("rmsw_b", [128, D], F32))
                stgB6 = b6_ctx.enter_context(nc.sbuf_tensor("stgB6", [128, D], F32))
                stg2A6 = b6_ctx.enter_context(nc.sbuf_tensor("stg2A6", [128, D], F32))
                stg2B6 = b6_ctx.enter_context(nc.sbuf_tensor("stg2B6", [128, D], F32))
                st6 = [stage, stgB6]
                st26 = [stg2A6, stg2B6]
                trd6 = {0: None, 1: None}
                spl6 = {0: None, 1: None}
                bcast_row(c_rmsw, D, 4, rmsw_b, waits=[cin0])
                for c in range(NCH):
                    par = c % 2
                    stg = st6[par]
                    stg2 = st26[par]
                    cb = par * 4
                    sq = SC(("act", stg[:], atm_t(c), ACT.Square, 0.0, 1.0),
                            waits=[trd6[par]])
                    V(("red", colbuf[:][:, cb:cb + 1], stg[:], OP.add, False),
                      waits=[sq])
                    VD()
                    V(("ts", colbuf[:][:, cb + 1:cb + 2], colbuf[:][:, cb:cb + 1],
                       1.0 / D, 1e-5, OP.mult, OP.add))
                    sq2 = SC(("act", colbuf[:][:, cb + 2:cb + 3],
                              colbuf[:][:, cb + 1:cb + 2],
                              ACT.Sqrt, 0.0, 1.0), waits=[(s_ve, pl.cur(s_ve))])
                    V(("recip", colbuf[:][:, cb + 3:cb + 4],
                       colbuf[:][:, cb + 2:cb + 3]), waits=[sq2])
                    VD()
                    V(("ts", stg[:], atm_t(c), colbuf[:][:, cb + 3:cb + 4], None,
                       OP.mult, OP.bypass))
                    VD()
                    V(("tt", stg[:], stg[:], rmsw_b[:], OP.mult))
                    VD()
                    anv = (s_ve, pl.cur(s_ve))
                    if "attn_n" in dbg:
                        GD(dbg["attn_n"][c], stg[:], s_dbg, waits=[anv])
                    for dc in range(NDC):
                        bk = dc % 4
                        TR(pbs[bk][:, 0:128], bk,
                           stg[:][:, dc * 128:(dc + 1) * 128], c_identf[:],
                           waits=[anv] if dc == 0 else ())
                        eng6 = "vector" if dc % 2 == 0 else "scalar"
                        EVAC(eng6, (("copy" if dc % 2 == 0 else "scopy"),
                                    stg2[:][:, dc * 128:(dc + 1) * 128],
                                    pbs[bk][:, 0:128]), bk,
                             waits=[spl6[par]] if (dc < 2 and spl6[par]) else ())
                    trd6[par] = (s_mm, pl.cur(s_mm))
                    gv = GD(afm_d[:, :, c * 128:(c + 1) * 128]
                            .rearrange("a p t -> p a t"),
                            stg2[:].rearrange("p (a t) -> p a t", a=NDC),
                            s_spC if par == 0 else s_spB,
                            waits=[(s_ve, pl.cur(s_ve)),
                                   (s_sc, pl.cur(s_sc))])
                    spl6[par] = gv
                    pl.defer("gpsimd", [(s_dbg, pl.cur(s_dbg))])
                    if c % 4 == 3:
                        pl.flush("B6")
                FENCE_ALL()
                b6_ctx.close()

            FENCE_ALL()
            atm_ctx.close()
        attn_ctx.close()
        rows_ctx = ExitStack()
        rows = rows_ctx.enter_context(nc.sbuf_tensor("rows", [1, 6 * 1024], F32))
        b7_ctx = ExitStack()
        ahi7 = b7_ctx.enter_context(nc.sbuf_tensor("ahi7", [128, NDC * TPC], BF16))
        alo7 = b7_ctx.enter_context(nc.sbuf_tensor("alo7", [128, NDC * TPC], BF16))
        stg7 = [b7_ctx.enter_context(nc.sbuf_tensor(f"stg7{i}", [128, D], F32))
                for i in range(2)]
        stq7 = [b7_ctx.enter_context(nc.sbuf_tensor(f"stq7{i}", [128, D], F32))
                for i in range(2)]
        if True:
            b7done = {0: None, 1: None}
            b7sp = {0: None, 1: None}
            FENCE((s_spC, pl.cur(s_spC)))
            pl.flush("B7")
            afm_ctx2 = ExitStack()
            afm = afm_ctx2.enter_context(nc.sbuf_tensor("afm", [128, NDC * TPC], F32))
            afl7 = {}
            spld7 = {}
            afl7[0] = SCY(afm[:][:, 0:TPC], afm_d[0], semx=s_w5)
            afl7[1] = SCY(afm[:][:, TPC:2 * TPC], afm_d[1], semx=s_w6)
            V(("memset", rows[:][:, 0:2 * 1024], 0.0))
            rz = (s_ve, pl.cur(s_ve))
            for dc in range(NDC):
                V(("copy", ahi7[:][:, dc * TPC:(dc + 1) * TPC],
                   afm[:][:, dc * TPC:(dc + 1) * TPC]),
                  waits=[afl7[dc]])
                VD()
                V(("tt", alo7[:][:, dc * TPC:(dc + 1) * TPC],
                   afm[:][:, dc * TPC:(dc + 1) * TPC],
                   ahi7[:][:, dc * TPC:(dc + 1) * TPC], OP.subtract))
                VD()
                spld7[dc] = (s_ve, pl.cur(s_ve))
                if dc + 2 < NDC:
                    afl7[dc + 2] = SCY(afm[:][:, (dc + 2) * TPC:(dc + 3) * TPC],
                                       afm_d[dc + 2],
                                       semx=s_w5 if dc % 2 == 0 else s_w6,
                                       waits=[spld7[dc]])
            afsv = (s_ve, pl.cur(s_ve))
            pl.flush("B7s")
            FENCE_ALL()
            afm_ctx2.close()
            w7_ctx = ExitStack()
            w7f = [w7_ctx.enter_context(nc.sbuf_tensor(f"w7f{i}", [128, NDC * 256], F32))
                   for i in range(2)]
            w7h = [w7_ctx.enter_context(nc.sbuf_tensor(f"w7h{i}", [128, NDC * 256], BF16))
                   for i in range(2)]
            w7l = [w7_ctx.enter_context(nc.sbuf_tensor(f"w7l{i}", [128, NDC * 256], BF16))
                   for i in range(2)]
            lw7 = {}
            spl7 = {}
            mmc7 = {}

            def b7_load(j):
                par = j % 2
                lw7[j] = SY(w7f[par][:].rearrange("p (c e) -> p c e", c=NDC),
                            ap["wo"][:, j * 256:(j + 1) * 256]
                            .rearrange("(c p) e -> p c e", p=128),
                            semx=s_w if par == 0 else s_w2,
                            waits=[spl7.get(j - 2)])

            def b7_split(j):
                par = j % 2
                V(("copy", w7h[par][:], w7f[par][:]),
                  waits=[lw7[j], mmc7.get(j - 2)])
                VD()
                V(("tt", w7l[par][:], w7f[par][:], w7h[par][:], OP.subtract))
                VD()
                spl7[j] = (s_ve, pl.cur(s_ve))

            pend_stat = [None]

            def flush_stat():
                if pend_stat[0]:
                    pend_stat[0]()
                    pend_stat[0] = None

            b7_load(0)
            b7_load(1)
            b7_split(0)
            for j in range(8):
                par8 = j % 2
                if j + 2 < 8:
                    b7_load(j + 2)
                if j + 1 < 8:
                    b7_split(j + 1)
                for ebl2 in range(2):
                    eb = j * 2 + ebl2
                    par = eb % 2
                    stg = stg7[par]
                    stq = stq7[par]
                    for sl in range(NSL):
                        bk = par * 2 + sl
                        for dc in range(NDC):
                            wth = w7h[par8][:][:, dc * 256 + ebl2 * 128:
                                               dc * 256 + (ebl2 + 1) * 128]
                            wtl = w7l[par8][:][:, dc * 256 + ebl2 * 128:
                                               dc * 256 + (ebl2 + 1) * 128]
                            ah = ahi7[:][:, dc * TPC + sl * SL:
                                         dc * TPC + (sl + 1) * SL]
                            al = alo7[:][:, dc * TPC + sl * SL:
                                         dc * TPC + (sl + 1) * SL]
                            MM(pbs[bk][:, 0:SL], bk, wth, ah, dc == 0, False,
                               waits=[spl7[j], afsv] if dc == 0 else ())
                            MM(pbs[bk][:, 0:SL], bk, wth, al, False, False)
                            MM(pbs[bk][:, 0:SL], bk, wtl, ah, False,
                               dc == NDC - 1)
                    flush_stat()
                    lx = GD(stq[:][:, 0:TPC], x_fm_d[eb], s_ld2,
                            waits=[b7done[par]] if b7done[par] else ())
                    for sl in range(NSL):
                        bk = par * 2 + sl
                        EVAC("vector",
                             ("tt", stg[:][:, sl * SL:(sl + 1) * SL],
                              stq[:][:, sl * SL:(sl + 1) * SL], pbs[bk][:, 0:SL],
                              OP.add), bk,
                             waits=[lx] + ([b7sp[par], b7done[par]]
                                           if (sl == 0 and b7sp[par]) else []))
                    yev = (s_ve, pl.cur(s_ve))
                    gv = GD(y1_d[eb], stg[:][:, 0:TPC],
                            s_spD if par == 0 else s_spF, waits=[yev])
                    if "y1" in dbg:
                        GD(dbg["y1"][eb], stg[:][:, 0:TPC], s_dbg, waits=[yev])
                    sq = SC(("act", stq[:][:, 0:TPC], stg[:][:, 0:TPC],
                             ACT.Square, 0.0, 1.0), waits=[yev])

                    def mk_stat(eb=eb, par=par, yev=yev, sq=sq,
                                stg=stg, stq=stq):
                        def f():
                            for sl in range(NSL):
                                MM(pbs[4 + sl][0:1, 0:SL], 4 + sl, c_onesc[:],
                                   stg[:][:, sl * SL:(sl + 1) * SL],
                                   eb == 0, eb == 15, waits=[yev],
                                   mark=True)
                                MM(pbs[6 + sl][0:1, 0:SL], 6 + sl, c_onesc[:],
                                   stq[:][:, sl * SL:(sl + 1) * SL],
                                   eb == 0, eb == 15, waits=[sq],
                                   mark=True)
                            b7done[par] = (s_mm, pl.cur(s_mm))
                        return f
                    pend_stat[0] = mk_stat()
                    b7sp[par] = gv
                    pl.defer("gpsimd", [(s_dbg, pl.cur(s_dbg))])
                mmc7[j] = (s_mm, pl.cur(s_mm))
                if j % 4 == 3:
                    pl.flush("B7")
            flush_stat()
            for sl in range(NSL):
                EVAC("vector", ("copy", rows[:][:, sl * SL:(sl + 1) * SL],
                                pbs[4 + sl][0:1, 0:SL]), 4 + sl)
                EVAC("vector",
                     ("copy", rows[:][:, 1024 + sl * SL:1024 + (sl + 1) * SL],
                      pbs[6 + sl][0:1, 0:SL]), 6 + sl)
            pl.flush("B7")
            FENCE_ALL()
            w7_ctx.close()
        FENCE_ALL()
        b7_ctx.close()
        V(("ts", rows[:][:, 2 * 1024:2 * 1024 + TPC], rows[:][:, 0:TPC],
           1.0 / D, None, OP.mult, OP.bypass))
        V(("ts", rows[:][:, 3 * 1024:3 * 1024 + TPC], rows[:][:, 1024:1024 + TPC],
           1.0 / D, None, OP.mult, OP.bypass))
        VD()
        V(("tt", rows[:][:, 4 * 1024:4 * 1024 + TPC],
           rows[:][:, 2 * 1024:2 * 1024 + TPC],
           rows[:][:, 2 * 1024:2 * 1024 + TPC], OP.mult))
        VD()
        V(("tt", rows[:][:, 3 * 1024:3 * 1024 + TPC],
           rows[:][:, 3 * 1024:3 * 1024 + TPC],
           rows[:][:, 4 * 1024:4 * 1024 + TPC], OP.subtract))
        VD()
        V(("ts", rows[:][:, 3 * 1024:3 * 1024 + TPC],
           rows[:][:, 3 * 1024:3 * 1024 + TPC], 1.0, 1e-5, OP.mult, OP.add))
        sqv = SC(("act", rows[:][:, 4 * 1024:4 * 1024 + TPC],
                  rows[:][:, 3 * 1024:3 * 1024 + TPC], ACT.Sqrt, 0.0, 1.0),
                 waits=[(s_ve, pl.cur(s_ve))])
        V(("recip", rows[:][:, 3 * 1024:3 * 1024 + TPC],
           rows[:][:, 4 * 1024:4 * 1024 + TPC]), waits=[sqv])
        VD()
        V(("tt", rows[:][:, 4 * 1024:4 * 1024 + TPC],
           rows[:][:, 2 * 1024:2 * 1024 + TPC],
           rows[:][:, 3 * 1024:3 * 1024 + TPC], OP.mult))
        VD()
        V(("ts", rows[:][:, 4 * 1024:4 * 1024 + TPC],
           rows[:][:, 4 * 1024:4 * 1024 + TPC], -1.0, None, OP.mult, OP.bypass))
        lnv = (s_ve, pl.cur(s_ve))
        GD(ln1_d[:, 0:TPC], rows[:][:, 3 * 1024:3 * 1024 + TPC], s_spD,
           waits=[lnv])
        GD(ln1_d[:, 1024:1024 + TPC], rows[:][:, 4 * 1024:4 * 1024 + TPC], s_spD)
        if "ln1" in dbg:
            GD(dbg["ln1"][:, 0:1024], rows[:][:, 3 * 1024:4 * 1024], s_dbg)
            GD(dbg["ln1"][:, 1024:2048], rows[:][:, 4 * 1024:5 * 1024], s_dbg)
        FENCE_ALL()
        rows_ctx.close()
        pl.flush("B7")

        b9_ctx = ExitStack()
        ghi = b9_ctx.enter_context(nc.sbuf_tensor("ghi", [128, NHC * TQ], F32R))
        glo = b9_ctx.enter_context(nc.sbuf_tensor("glo", [128, NHC * TQ], F32R))
        x2hi = b9_ctx.enter_context(nc.sbuf_tensor("x2hi", [128, NDC * TQ], F32R))
        x2lo = b9_ctx.enter_context(nc.sbuf_tensor("x2lo", [128, NDC * TQ], F32R))
        wsA = b9_ctx.enter_context(nc.sbuf_tensor("wsA", [128, HIDP], F32R))
        wsB = b9_ctx.enter_context(nc.sbuf_tensor("wsB", [128, HIDP], F32R))
        yqb = b9_ctx.enter_context(nc.sbuf_tensor("yqb", [128, NDC * TQ], F32))
        rowsq = b9_ctx.enter_context(nc.sbuf_tensor("rowsq", [1, 5 * TQ], F32))
        y1pc = [b9_ctx.enter_context(nc.sbuf_tensor(f"y1pc{i}", [128, TQ], F32))
                for i in range(2)]
        bca2 = b9_ctx.enter_context(nc.sbuf_tensor("bca2", [128, TQ], F32))
        bcb2 = b9_ctx.enter_context(nc.sbuf_tensor("bcb2", [128, TQ], F32))
        x2sc = b9_ctx.enter_context(nc.sbuf_tensor("x2sc", [128, TQ], F32))
        if True:
            wbufs = [wsA, wsB]
            csem = [s_cast, s_cast2]
            FENCE((s_spD, pl.cur(s_spD)), (s_spE, pl.cur(s_spE)),
                  (s_spF, pl.cur(s_spF)))
            pl.flush("B9h")
            pl.defer("gpsimd", [(s_cc, pl.cur(s_cc))])
            consW = {0: None, 1: None}
            x2con = None
            ghcon = None
            yqcon = None
            rqcon = None
            x2v = None
            bcaP = [bca, bca2]
            bcbP = [bcb, bcb2]
            lyq = None

            def t1v(hb):
                return cct_out[hb % NCORES][(hb // NCORES) * NDC:
                                            (hb // NCORES + 1) * NDC]

            def t3v(hb):
                return cct_out[hb % NCORES][S1 * NDC + (hb // NCORES) * NDC:
                                            S1 * NDC + (hb // NCORES + 1) * NDC]

            def t2v(db):
                b0 = 2 * S1 * NDC + (db // NCORES) * NHC
                return cct_out[db % NCORES][b0:b0 + NHC]

            def load_g_strips(hb, par):
                pl.emit("gpsimd",
                        ("dma", wbufs[par][:][:, 0:NDC * 128]
                         .rearrange("p (a t) -> p a t", a=NDC),
                         t1v(hb).rearrange("a p t -> p a t")),
                        [consW[par]] if consW[par] else [], [(csem[par], 16)])
                pl.emit("gpsimd",
                        ("dma", wbufs[par][:][:, NDC * 128:2 * NDC * 128]
                         .rearrange("p (a t) -> p a t", a=NDC),
                         t3v(hb).rearrange("a p t -> p a t")),
                        [], [(csem[par], 16)])
                return (csem[par], pl.cur(csem[par]))

            def load_y_strip(db, par):
                pl.emit("gpsimd",
                        ("dma", wbufs[par][:][:, 0:NHC * 128]
                         .rearrange("p (a t) -> p a t", a=NHC),
                         t2v(db).rearrange("a p t -> p a t")),
                        [consW[par]] if consW[par] else [], [(csem[par], 16)])
                return (csem[par], pl.cur(csem[par]))

            def ln1_rows(tq):
                tq0 = tq * TQ
                pl.emit("gpsimd", ("dma", rowsq[:, 0:TQ], ln1_d[:, tq0:tq0 + TQ]),
                        [rqcon], [(s_ld3, 16)])
                pl.emit("gpsimd", ("dma", rowsq[:, TQ:2 * TQ],
                                   ln1_d[:, 1024 + tq0:1024 + tq0 + TQ]),
                        [], [(s_ld3, 16)])
                lr = (s_ld3, pl.cur(s_ld3))
                bcast_row(rowsq[:, 0:TQ], TQ, 6, bcaP[tq % 2], waits=[lr])
                bcast_row(rowsq[:, TQ:2 * TQ], TQ, 6, bcbP[tq % 2])
                return (s_mm, pl.cur(s_mm)), (s_ve, pl.cur(s_ve))

            def x2_prep(tq, xwar):
                tq0 = tq * TQ
                ldp = {}
                pcon = {0: None, 1: None}
                pwar = list(x2v) if x2v else []
                ldp[0] = GD(y1pc[0][:], y1_d[:, :, tq0:tq0 + TQ]
                            .rearrange("a p t -> p a t")[:, 0], s_ld1,
                            waits=pwar)
                ldp[1] = GD(y1pc[1][:], y1_d[:, :, tq0:tq0 + TQ]
                            .rearrange("a p t -> p a t")[:, 1], s_ld2,
                            waits=pwar)
                for dc in range(NDC):
                    par = dc % 2
                    V(("tt", gtmp2[:], y1pc[par][:], bcaP[tq % 2][:], OP.mult),
                      waits=[ldp[dc]] + ([xwar, bcv2] if dc == 0 else []))
                    VD()
                    V(("tt", gtmp2[:], gtmp2[:], bcbP[tq % 2][:], OP.add))
                    VD()
                    V(("copy", x2hi[:][:, dc * TQ:(dc + 1) * TQ], gtmp2[:]))
                    VD()
                    V(("tt", x2lo[:][:, dc * TQ:(dc + 1) * TQ], gtmp2[:],
                       x2hi[:][:, dc * TQ:(dc + 1) * TQ].bitcast(F32),
                       OP.subtract))
                    VD()
                    pcon[par] = (s_ve, pl.cur(s_ve))
                    if dc + 2 < NDC:
                        ldp[dc + 2] = GD(y1pc[par][:],
                                         y1_d[:, :, tq0:tq0 + TQ]
                                         .rearrange("a p t -> p a t")[:, dc + 2],
                                         s_ld1 if par == 0 else s_ld2,
                                         waits=[pcon[par]])
                return [(s_ve, pl.cur(s_ve)), (s_gp, pl.cur(s_gp))]

            rqcon, bcv2 = ln1_rows(0)
            x2v = x2_prep(0, None)
            lyq = GD(yqb[:].rearrange("p (a t) -> p a t", a=NDC),
                     y1_d[:, :, 0:TQ].rearrange("a p t -> p a t"), s_spF)
            lcur = load_g_strips(0, 0)
            pl.flush("B9p")
            for tq in range(NTQ):
                tq0 = tq * TQ
                for hb in range(NHC):
                    par = hb % 2
                    if hb + 1 < NHC:
                        lnext = load_g_strips(hb + 1, 1 - par)
                    buf = wbufs[par][:]
                    b1, b3 = hb % 2, 2 + hb % 2
                    for dc in range(NDC):
                        MM(pbs[b1][:, 0:TQ], b1,
                           buf[:, dc * 128:(dc + 1) * 128],
                           x2hi[:][:, dc * TQ:(dc + 1) * TQ],
                           dc == 0, False,
                           waits=([lcur] + x2v) if dc == 0 else ())
                        MM(pbs[b1][:, 0:TQ], b1,
                           buf[:, dc * 128:(dc + 1) * 128],
                           x2lo[:][:, dc * TQ:(dc + 1) * TQ],
                           False, dc == NDC - 1)
                        MM(pbs[b3][:, 0:TQ], b3,
                           buf[:, NDC * 128 + dc * 128:NDC * 128 + (dc + 1) * 128],
                           x2hi[:][:, dc * TQ:(dc + 1) * TQ],
                           dc == 0, False)
                        MM(pbs[b3][:, 0:TQ], b3,
                           buf[:, NDC * 128 + dc * 128:NDC * 128 + (dc + 1) * 128],
                           x2lo[:][:, dc * TQ:(dc + 1) * TQ],
                           False, dc == NDC - 1,
                           mark=(dc == NDC - 1 and hb == NHC - 1))
                    consW[par] = (s_mm, pl.cur(s_mm))
                    if hb == NHC - 1:
                        x2con = (s_mm, pl.cur(s_mm))
                    if hb + 1 < NHC:
                        lcur = lnext
                    sv = EVAC("scalar", ("act", gtmp[:], pbs[b1][:, 0:TQ], ACT.Silu,
                                         0.0, scol[:][:, 0:1]), b1,
                              waits=[(s_ve, pl.cur(s_ve))])
                    gv2 = EVAC("vector", ("stt", gtmp2[:], pbs[b3][:, 0:TQ],
                                          scol[:][:, 1:2], gtmp[:], OP.mult,
                                          OP.mult), b3, waits=[sv])
                    VD()
                    V(("copy", ghi[:][:, hb * TQ:(hb + 1) * TQ], gtmp2[:]),
                      waits=[ghcon] if hb == 0 else ())
                    VD()
                    V(("tt", glo[:][:, hb * TQ:(hb + 1) * TQ], gtmp2[:],
                       ghi[:][:, hb * TQ:(hb + 1) * TQ].bitcast(F32), OP.subtract))
                    VD()
                    if hb == 20:
                        pl.flush(f"B9G_{tq}")
                gfv = (s_ve, pl.cur(s_ve))
                if "gq" in dbg:
                    for hb in range(NHC):
                        V(("copy", gtmp[:], ghi[:][:, hb * TQ:(hb + 1) * TQ]
                           .bitcast(F32)), waits=[gfv] if hb == 0 else ())
                        VD()
                        V(("tt", gtmp[:], gtmp[:],
                           glo[:][:, hb * TQ:(hb + 1) * TQ].bitcast(F32), OP.add))
                        GD(dbg["gq"][tq][hb], gtmp[:],
                           s_dbg, waits=[(s_ve, pl.cur(s_ve))])
                        pl.defer("vector", [(s_dbg, pl.cur(s_dbg))])
                    pl.defer("scalar", [(s_dbg, pl.cur(s_dbg))])
                if tq + 1 < NTQ:
                    rqcon, bcv2 = ln1_rows(tq + 1)

                pend9 = [None]

                def flush9():
                    if pend9[0]:
                        pend9[0]()
                        pend9[0] = None
                ycur = load_y_strip(0, 0)
                for db in range(NDC):
                    par = db % 2
                    bank = (db // 2) % 4
                    sub = db % 2
                    if db + 1 < NDC:
                        ynext = load_y_strip(db + 1, 1 - par)
                    buf = wbufs[par][:]
                    for hc in range(NHC):
                        MM(pbs[bank][:, sub * TQ:(sub + 1) * TQ], bank,
                           buf[:, hc * 128:(hc + 1) * 128],
                           ghi[:][:, hc * TQ:(hc + 1) * TQ],
                           (sub == 0 and hc == 0), False,
                           waits=[ycur, gfv] if hc == 0 else ())
                        MM(pbs[bank][:, sub * TQ:(sub + 1) * TQ], bank,
                           buf[:, hc * 128:(hc + 1) * 128],
                           glo[:][:, hc * TQ:(hc + 1) * TQ],
                           False, (sub == 1 and hc == NHC - 1),
                           mark=(sub == 0 and hc == NHC - 1))
                    consW[par] = (s_mm, pl.cur(s_mm))
                    if db == NDC - 1:
                        ghcon = (s_mm, pl.cur(s_mm))
                    if db + 1 < NDC:
                        ycur = ynext
                    if sub == 1:
                        flush9()
                        for s2 in range(2):
                            db2 = (db // 2) * 2 + s2
                            V(("tt", gtmp2[:], yqb[:][:, db2 * TQ:(db2 + 1) * TQ],
                               bcaP[tq % 2][:], OP.mult),
                              waits=[lyq] if (bank == 0 and s2 == 0) else ())
                            VD()
                            V(("tt", gtmp2[:], gtmp2[:], bcbP[tq % 2][:], OP.add))
                            VD()
                            EVAC("vector",
                                 ("stt", yqb[:][:, db2 * TQ:(db2 + 1) * TQ],
                                  pbs[bank][:, s2 * TQ:(s2 + 1) * TQ],
                                  scol[:][:, 2:3], gtmp2[:],
                                  OP.mult, OP.add), bank)
                            VD()
                        yev9 = (s_ve, pl.cur(s_ve))

                        def mk9(bank=bank, yev9=yev9, pb=(db // 2) * 2):
                            def f():
                                for s2 in range(2):
                                    db2 = pb + s2
                                    sq9 = SC(("act", gtmp[:],
                                              yqb[:][:, db2 * TQ:(db2 + 1) * TQ],
                                              ACT.Square, 0.0, 1.0), waits=[yev9])
                                    MM(pbs[6][0:1, 0:TQ], 6, c_onesc[:],
                                       yqb[:][:, db2 * TQ:(db2 + 1) * TQ],
                                       db2 == 0, False, waits=[yev9],
                                       mark=True)
                                    MM(pbs[6][0:1, TQ:2 * TQ], 6, c_onesc[:],
                                       gtmp[:], False, db2 == NDC - 1,
                                       waits=[sq9], mark=True)
                                    pl.defer("scalar",
                                             [(s_mm, pl.cur(s_mm))])
                            return f
                        pend9[0] = mk9()
                    if db == 9 and tq + 1 < NTQ:
                        x2v = x2_prep(tq + 1, x2con)
                flush9()
                RM, RV, RT = 2 * TQ, 3 * TQ, 4 * TQ
                EVAC("vector", ("ts", rowsq[:, RM:RM + TQ], pbs[6][0:1, 0:TQ],
                                1.0 / D, None, OP.mult, OP.bypass), 6)
                V(("ts", rowsq[:, RV:RV + TQ], pbs[6][0:1, TQ:2 * TQ],
                   1.0 / D, None, OP.mult, OP.bypass))
                VD()
                V(("tt", rowsq[:, RT:RT + TQ], rowsq[:, RM:RM + TQ],
                   rowsq[:, RM:RM + TQ], OP.mult))
                VD()
                V(("tt", rowsq[:, RV:RV + TQ], rowsq[:, RV:RV + TQ],
                   rowsq[:, RT:RT + TQ], OP.subtract))
                VD()
                V(("ts", rowsq[:, RV:RV + TQ], rowsq[:, RV:RV + TQ],
                   1.0, 1e-5, OP.mult, OP.add))
                sq3 = SC(("act", rowsq[:, RT:RT + TQ], rowsq[:, RV:RV + TQ],
                          ACT.Sqrt, 0.0, 1.0), waits=[(s_ve, pl.cur(s_ve))])
                V(("recip", rowsq[:, RV:RV + TQ], rowsq[:, RT:RT + TQ]),
                  waits=[sq3])
                VD()
                V(("tt", rowsq[:, RT:RT + TQ], rowsq[:, RM:RM + TQ],
                   rowsq[:, RV:RV + TQ], OP.mult))
                VD()
                V(("ts", rowsq[:, RT:RT + TQ], rowsq[:, RT:RT + TQ],
                   -1.0, None, OP.mult, OP.bypass))
                bkv = (s_ve, pl.cur(s_ve))
                for ttile in range(NTT):
                    MM(pbs[4 + ttile % 2][:, 0:128], 4 + ttile % 2,
                       rowsq[0:1, RV + ttile * 128:RV + (ttile + 1) * 128],
                       c_identf[0:1, 0:128], True, True,
                       waits=[bkv] if ttile == 0 else ())
                    EVAC("vector", ("copy", colbuf[:][:, 8 + ttile:9 + ttile],
                                    pbs[4 + ttile % 2][:, 0:1]), 4 + ttile % 2)
                    MM(pbs[4 + ttile % 2][:, 0:128], 4 + ttile % 2,
                       rowsq[0:1, RT + ttile * 128:RT + (ttile + 1) * 128],
                       c_identf[0:1, 0:128], True, True)
                    EVAC("vector", ("copy", colbuf[:][:, 12 + ttile:13 + ttile],
                                    pbs[4 + ttile % 2][:, 0:1]), 4 + ttile % 2)
                for ttile in range(NTT):
                    for db in range(NDC):
                        bko = 4 + db % 4
                        TR(pbs[bko][:, 0:128], bko,
                           yqb[:][:, db * TQ + ttile * 128: db * TQ + (ttile + 1) * 128],
                           c_identf[:])
                        EVAC("vector",
                             ("ts", stage[:][:, db * 128:(db + 1) * 128],
                              pbs[bko][:, 0:128],
                              colbuf[:][:, 8 + ttile:9 + ttile],
                              colbuf[:][:, 12 + ttile:13 + ttile],
                              OP.mult, OP.add), bko)
                    ov = (s_ve, pl.cur(s_ve))
                    GD(out_t[tq0 + ttile * 128: tq0 + (ttile + 1) * 128, :],
                       stage[:], s_outd, waits=[ov])
                    pl.defer("vector", [(s_outd, pl.cur(s_outd))])
                yqcon = (s_mm, pl.cur(s_mm))
                if tq + 1 < NTQ:
                    lyq = GD(yqb[:].rearrange("p (a t) -> p a t", a=NDC),
                             y1_d[:, :, tq0 + TQ:tq0 + 2 * TQ]
                             .rearrange("a p t -> p a t"), s_spF,
                             waits=[yqcon])
                    lcur = load_g_strips(0, 0)

        FENCE_ALL()
        b9_ctx.close()
        pl.emit("gpsimd", ("nop",), [(s_outd, pl.cur(s_outd)), (s_dbg, pl.cur(s_dbg)), (s_spD, pl.cur(s_spD)),
                                     (s_cast, pl.cur(s_cast))], [])
        pl.flush(f"B9L_{tq}")

    return nc



def make_inputs(x_full, Wq, Wk, Wv, Wo, rms_w, w1, w3, w2, NCH=8, NCORES=8):
    TPC = NCH * CH
    cdc = _decay_constants()
    dmaskT = cdc["dmask"].transpose(0, 2, 1).transpose(1, 0, 2) \
        .reshape(128, NH * CH).copy().astype(np.float32)
    qdec = np.broadcast_to(cdc["q_decay"][:, None, :], (NH, 128, CH)) \
        .transpose(1, 0, 2).reshape(128, NH * CH).copy().astype(np.float32)
    kdec = np.broadcast_to(cdc["k_decay"][:, None, :], (NH, 128, CH)) \
        .transpose(1, 0, 2).reshape(128, NH * CH).copy().astype(np.float32)
    swap = np.zeros((128, 128), dtype=np.float32)
    for dd in range(128):
        swap[(dd + 64) % 128, dd] = 1.0
    r64 = cdc["r"].astype(np.float64)
    S1 = -(-NHC // NCORES)
    S2 = -(-NDC // NCORES)

    def rowshard(w, nslot, core):
        out = np.zeros((nslot * 128, w.shape[1]), dtype=np.float32)
        for sl_ in range(nslot):
            r0 = (sl_ * NCORES + core) * 128
            r1 = min(r0 + 128, w.shape[0])
            if r0 < w.shape[0]:
                out[sl_ * 128:sl_ * 128 + (r1 - r0)] = w[r0:r1]
        return out
    ins = []
    for core in range(NCORES):
        cos, sin = _rotary_tables(core * TPC, TPC)
        rotA = np.concatenate([cos.T, cos.T], axis=0).astype(np.float32)
        rotB = np.concatenate([sin.T, -sin.T], axis=0).astype(np.float32)
        coef = np.zeros((NCORES, NH), dtype=np.float32)
        for j in range(NCORES):
            if j < core:
                coef[j] = (r64 ** (128.0 * 8.0 * (core - 1 - j))).astype(np.float32)
        coef_b = np.broadcast_to(coef.reshape(1, -1),
                                 (128, NCORES * NH)).copy().astype(np.float32)
        d = dict(
            x=np.ascontiguousarray(x_full[core * TPC:(core + 1) * TPC]),
            wq=Wq, wk=Wk, wv=Wv, wo=Wo,
            rmsw_row=np.ascontiguousarray(rms_w.reshape(1, D)),
            w1s=rowshard(w1, S1, core), w3s=rowshard(w3, S1, core),
            w2s=rowshard(w2, S2, core),
            identf=np.eye(128, dtype=np.float32),
            identb=np.eye(128, dtype=ml_dtypes.bfloat16),
            swap64=swap,
            rotA=rotA, rotB=rotB,
            dmaskT=dmaskT, qdec=qdec, kdec=kdec,
            coef=coef_b,
            onescol=np.ones((128, 1), dtype=np.float32),
            onesrow=np.ones((1, 128), dtype=np.float32),
        )
        ins.append(d)
    return ins


_NC_CACHE = {}


def kernel(L0_attn_x, L0_attn_Wq, L0_attn_Wk, L0_attn_Wv, L0_attn_Wo,
           L0_attn_rms_w, L0_ffn_w1, L0_ffn_w3, L0_ffn_w2):
    from concourse.bass_utils import run_bass_kernel_spmd
    f32 = lambda a: np.ascontiguousarray(np.asarray(a, dtype=np.float32))
    x = f32(L0_attn_x).reshape(8192, D)
    ins = make_inputs(x, f32(L0_attn_Wq), f32(L0_attn_Wk), f32(L0_attn_Wv),
                      f32(L0_attn_Wo), f32(L0_attn_rms_w), f32(L0_ffn_w1),
                      f32(L0_ffn_w3), f32(L0_ffn_w2), NCH=8, NCORES=8)
    if "nc" not in _NC_CACHE:
        _NC_CACHE["nc"] = build(NCH=8, NCORES=8)
    res = run_bass_kernel_spmd(_NC_CACHE["nc"], ins, core_ids=list(range(8)))
    out = np.concatenate([res.results[c]["out"] for c in range(8)], axis=0)
    return out.reshape(1, 8192, D)
